# version 06
# baseline (speedup 1.0000x reference)
import contextlib
import math
import numpy as np


class StopBuild(Exception):
    pass

import concourse.bass as bass
import concourse.mybir as mybir
from concourse.bass_utils import run_bass_kernel_spmd

F32 = mybir.dt.float32
BF16 = mybir.dt.bfloat16
U32 = mybir.dt.uint32
AF = mybir.ActivationFunctionType
ALU = mybir.AluOpType
AX = mybir.AxisListType

T = 2048
C = 2048
H = 16
G = 4
FF = 1408
E = 8
MC = 11
NT = 16
KC = 16
QSCALE = 1.0 / math.sqrt(128.0)
NEG = -30000.0
EPS = 1e-5


PHASES = ["b0", "b1", "b2", "b3", "b4", "b6", "b7", "b8"]


def build(debug=(), stop="b8"):
    want = lambda p: PHASES.index(p) <= PHASES.index(stop)
    nc = bass.Bass("TRN2", target_bir_lowering=False)

    x_d = nc.dram_tensor("x", [T, C], F32, kind="ExternalInput")
    wq_d = nc.dram_tensor("wq", [C, C], F32, kind="ExternalInput")
    wkv_d = nc.dram_tensor("wkv", [C, 1024], F32, kind="ExternalInput")
    wo_d = nc.dram_tensor("wo", [C, C], F32, kind="ExternalInput")
    bo_d = nc.dram_tensor("bo", [1, C], F32, kind="ExternalInput")
    gw_d = nc.dram_tensor("gw", [C, E], F32, kind="ExternalInput")
    wg_d = nc.dram_tensor("wg", [E, C, FF], F32, kind="ExternalInput")
    wu_d = nc.dram_tensor("wu", [E, C, FF], F32, kind="ExternalInput")
    wd_d = nc.dram_tensor("wd", [E, FF, C], F32, kind="ExternalInput")
    alibi_d = nc.dram_tensor("alibi", [H, T], F32, kind="ExternalInput")
    mask_d = nc.dram_tensor("mask", [128, 128], BF16, kind="ExternalInput")
    out_d = nc.dram_tensor("out", [T, C], F32, kind="ExternalOutput")
    ao_s = nc.dram_tensor("ao_s", [T, C], BF16, kind="Internal")
    y1f_s = nc.dram_tensor("y1f_s", [T, C], F32, kind="Internal")
    y1b_s = nc.dram_tensor("y1b_s", [T, C], BF16, kind="Internal")
    comb_s = nc.dram_tensor("comb_s", [E, T], F32, kind="Internal")
    a_s = nc.dram_tensor("a_s", [NT, 128, E, MC, 128], BF16, kind="Internal")
    wdn_s = nc.dram_tensor("wdn_s", [E, MC, 128, C], BF16, kind="Internal")

    dbg = {}
    for name, shape, dt in [
        ("dbg_q", [128, H * T], F32), ("dbg_k", [128, G * T], F32),
        ("dbg_vt", [128, NT * 512], F32), ("dbg_ao", [T, C], F32),
        ("dbg_y1", [T, C], F32), ("dbg_comb", [E, T], F32),
    ]:
        if name in debug:
            dbg[name] = nc.dram_tensor(name, shape, dt, kind="ExternalOutput")

    es = contextlib.ExitStack()
    SB = lambda n, s, d: es.enter_context(nc.sbuf_tensor(n, s, d))
    SEM = lambda n: es.enter_context(nc.semaphore(n))

    try:
        with es:
            ident_b = SB("ident_b", [128, 128], BF16)
            ident_f = SB("ident_f", [128, 128], F32)
            ones_b = SB("ones_b", [1, 128], BF16)
            zero_b = SB("zero_b", [128, 512], BF16)
            mask_sb = SB("mask_sb", [128, 128], BF16)
            bo_hi = SB("bo_hi", [1, C], BF16)
            bo_lo = SB("bo_lo", [1, C], BF16)
            gate_sb = SB("gate_sb", [128, KC * E], BF16)
            iota8 = SB("iota8", [128, E], F32)

            s_names = [
                "SY0", "GP0", "SY_x", "ACT_cast", "PE_t1", "DVE_te1",
                "GP_w", "PE_p2", "DVE_p2e", "PE_vt", "DVE_vte",
                "SY_bias", "PE_s", "DVE_ev", "DVE_mx", "DVE_mxp", "ACT_ex", "PE_tp",
                "DVE_te", "PE_av", "DVE_rc", "ACT_ae", "SY_ao",
                "SY_aot", "SY_xres", "PE_t5", "DVE_te5", "PE_wo", "DVE_r1",
                "DVE_bn", "ACT_rs", "DVE_y1", "SY_y1f", "GP_y1b",
                "SY_y1rd", "PE_gate", "DVE_rt", "ACT_rt", "PE_ct", "DVE_cte",
                "SY_comb",
                "GP_gw", "SY_cb", "PE_f1", "ACT_si", "DVE_uc", "DVE_a",
                "SY_aout", "GP_wdn", "SY_wdno",
                "SY_ain", "SY_wdn2", "SY_y1res", "PE_f2", "DVE_f2", "DVE_ln2",
                "ACT_rs2", "DVE_o", "SY_out", "SY_dbg",
            ]
            S = {n: SEM(n) for n in s_names}

            with contextlib.ExitStack() as e0:
                bo_sb = e0.enter_context(nc.sbuf_tensor("bo_sb", [1, C], F32))
                pcol = e0.enter_context(nc.sbuf_tensor("pcol", [128, 1], F32))
                iotaF = e0.enter_context(nc.sbuf_tensor("iotaF", [128, 128], F32))

                with nc.Block() as blk:
                    @blk.gpsimd
                    def _(g):
                        g.iota(pcol[:], [[0, 1]], channel_multiplier=1,
                               allow_small_or_imprecise_dtypes=True)
                        g.iota(iotaF[:], [[1, 128]], channel_multiplier=0,
                               allow_small_or_imprecise_dtypes=True)
                        g.iota(iota8[:], [[1, E]], channel_multiplier=0,
                               allow_small_or_imprecise_dtypes=True)
                        g.memset(ones_b[:], 1.0)
                        g.memset(zero_b[:], 0.0)
                        src = gw_d.ap().rearrange("(kc p) e -> p kc e", p=128)
                        dstg = gate_sb[:].rearrange("p (kc e) -> p kc e", kc=KC)
                        g.dma_start(dstg, src).then_inc(S["GP0"], 16)

                    @blk.sync
                    def _(sy):
                        sy.dma_start(mask_sb[:], mask_d.ap()).then_inc(S["SY0"], 16)
                        sy.dma_start(bo_sb[:], bo_d.ap()).then_inc(S["SY0"], 16)

                with nc.Block() as blk:
                    @blk.vector
                    def _(v):
                        v.wait_ge(S["SY0"], 32)
                        v.wait_ge(S["GP0"], 16)
                        v.tensor_scalar(ident_b[:], iotaF[:], pcol[:], None, ALU.is_equal)
                        v.tensor_scalar(ident_f[:], iotaF[:], pcol[:], None, ALU.is_equal)
                        v.tensor_copy(bo_hi[:], bo_sb[:])
                        v.tensor_sub(bo_lo[:], bo_sb[:], bo_hi[:])

            with contextlib.ExitStack() as esq:
                if not want("b1"):
                    raise StopBuild
                SBq = lambda n, s, d: esq.enter_context(nc.sbuf_tensor(n, s, d))
                PSq = lambda n, s, d: esq.enter_context(nc.psum_tensor(n, s, d))
                qf = SBq("qf", [128, H * T], BF16)
                kf = SBq("kf", [128, G * T], BF16)
                vt = SBq("vt", [128, NT * 512], BF16)

                with contextlib.ExitStack() as esx:
                    SBx = lambda n, s, d: esx.enter_context(nc.sbuf_tensor(n, s, d))
                    PSx = lambda n, s, d: esx.enter_context(nc.psum_tensor(n, s, d))
                    tp1 = [PSx(f"tp1_{i}", [128, 1024], BF16) for i in range(2)]
                    xT = SBx("xT", [128, KC * T], BF16)
                    vf = SBx("vf", [128, G * T], BF16)

                    with contextlib.ExitStack() as e1:
                        xbf = [e1.enter_context(nc.sbuf_tensor(f"xbf{i}", [128, C], BF16))
                               for i in range(2)]
                        with nc.Block() as blk:
                            @blk.gpsimd
                            def _(g):
                                for tt in range(NT):
                                    if tt >= 2:
                                        g.wait_ge(S["PE_t1"], 2 * tt - 2)
                                    g.dma_start(xbf[tt % 2][:],
                                                x_d.ap()[tt * 128:(tt + 1) * 128, :]
                                                ).then_inc(S["SY_x"], 16)

                            @blk.tensor
                            def _(t):
                                for tt in range(NT):
                                    t.wait_ge(S["SY_x"], 16 * (tt + 1))
                                    for b in range(2):
                                        gb = tt * 2 + b
                                        if gb >= 2:
                                            t.wait_ge(S["DVE_te1"], gb - 1)
                                        for j in range(8):
                                            c = b * 8 + j
                                            ins = t.transpose(
                                                tp1[gb % 2][:, j * 128:(j + 1) * 128],
                                                xbf[tt % 2][:, c * 128:(c + 1) * 128],
                                                ident_b[:])
                                        ins.then_inc(S["PE_t1"], 1)

                            @blk.vector
                            def _(v):
                                xTv = xT[:].rearrange("p (kc t) -> p kc t", kc=KC)
                                for tt in range(NT):
                                    for b in range(2):
                                        gb = tt * 2 + b
                                        v.wait_ge(S["PE_t1"], gb + 1)
                                        src = tp1[gb % 2][:].rearrange(
                                            "p (j t) -> p j t", j=8)
                                        dst = xTv[:, b * 8:b * 8 + 8,
                                                  tt * 128:(tt + 1) * 128]
                                        v.tensor_copy(dst, src).then_inc(S["DVE_te1"], 1)

                    with contextlib.ExitStack() as e2:
                        if not want("b2"):
                            raise StopBuild
                        wbuf = [e2.enter_context(
                            nc.sbuf_tensor(f"wbuf{i}", [128, KC * 256], BF16))
                            for i in range(2)]
                        pr = [e2.enter_context(
                            nc.psum_tensor(f"pr{i}", [128, 512], F32))
                            for i in range(2)]
                        with nc.Block() as blk:
                            @blk.gpsimd
                            def _(g):
                                for Q in range(12):
                                    if Q >= 2:
                                        g.wait_ge(S["PE_p2"], (Q - 1) * 8)
                                    if Q < 8:
                                        src = wq_d.ap()[:, Q * 256:(Q + 1) * 256]
                                    else:
                                        src = wkv_d.ap()[:, (Q - 8) * 256:(Q - 7) * 256]
                                    src = src.rearrange("(kc p) f -> p kc f", p=128)
                                    dst = wbuf[Q % 2][:].rearrange(
                                        "p (kc f) -> p kc f", kc=KC)
                                    g.dma_start(dst, src).then_inc(S["GP_w"], 16)

                            @blk.tensor
                            def _(t):
                                for Q in range(12):
                                    t.wait_ge(S["GP_w"], 16 * (Q + 1))
                                    for c in range(2):
                                        for ts in range(4):
                                            gi = Q * 8 + c * 4 + ts
                                            if gi >= 2:
                                                t.wait_ge(S["DVE_p2e"], gi - 1)
                                            for kc in range(KC):
                                                ins = t.matmul(
                                                    pr[gi % 2][:],
                                                    wbuf[Q % 2][:, kc * 256 + c * 128:
                                                                kc * 256 + (c + 1) * 128],
                                                    xT[:, kc * T + ts * 512:
                                                       kc * T + (ts + 1) * 512],
                                                    start=(kc == 0), stop=(kc == KC - 1))
                                            ins.then_inc(S["PE_p2"], 1)

                            @blk.vector
                            def _(v):
                                for Q in range(12):
                                    for c in range(2):
                                        for ts in range(4):
                                            gi = Q * 8 + c * 4 + ts
                                            v.wait_ge(S["PE_p2"], gi + 1)
                                            fo = Q * 2 + c
                                            if Q < 8:
                                                h = fo
                                                v.tensor_scalar_mul(
                                                    qf[:, h * T + ts * 512:
                                                       h * T + (ts + 1) * 512],
                                                    pr[gi % 2][:], QSCALE
                                                    ).then_inc(S["DVE_p2e"], 1)
                                            elif fo < 20:
                                                gg = fo - 16
                                                v.tensor_copy(
                                                    kf[:, gg * T + ts * 512:
                                                       gg * T + (ts + 1) * 512],
                                                    pr[gi % 2][:]
                                                    ).then_inc(S["DVE_p2e"], 1)
                                            else:
                                                gg = fo - 20
                                                v.tensor_copy(
                                                    vf[:, gg * T + ts * 512:
                                                       gg * T + (ts + 1) * 512],
                                                    pr[gi % 2][:]
                                                    ).then_inc(S["DVE_p2e"], 1)

                    with nc.Block() as blk:
                        @blk.tensor
                        def _(t):
                            for b in range(8):
                                if b >= 2:
                                    t.wait_ge(S["DVE_vte"], b - 1)
                                for j in range(8):
                                    i = b * 8 + j
                                    gg, ut = i // NT, i % NT
                                    ins = t.transpose(tp1[b % 2][:, j * 128:(j + 1) * 128],
                                                      vf[:, gg * T + ut * 128:
                                                         gg * T + (ut + 1) * 128],
                                                      ident_b[:])
                                ins.then_inc(S["PE_vt"], 1)

                        @blk.vector
                        def _(v):
                            for b in range(8):
                                v.wait_ge(S["PE_vt"], b + 1)
                                for j in range(8):
                                    i = b * 8 + j
                                    gg, ut = i // NT, i % NT
                                    cp = v.tensor_copy(
                                        vt[:, ut * 512 + gg * 128:
                                           ut * 512 + (gg + 1) * 128],
                                        tp1[b % 2][:, j * 128:(j + 1) * 128])
                                    if j == 7:
                                        cp.then_inc(S["DVE_vte"], 1)

                    if any(k in dbg for k in ("dbg_q", "dbg_k", "dbg_vt")):
                        with nc.Block() as blk:
                            @blk.gpsimd
                            def _(g):
                                if "dbg_q" in dbg:
                                    g.dma_start(dbg["dbg_q"].ap(), qf[:]
                                                ).then_inc(S["SY_dbg"], 16)
                                if "dbg_k" in dbg:
                                    g.dma_start(dbg["dbg_k"].ap(), kf[:]
                                                ).then_inc(S["SY_dbg"], 16)
                                if "dbg_vt" in dbg:
                                    g.dma_start(dbg["dbg_vt"].ap(), vt[:]
                                                ).then_inc(S["SY_dbg"], 16)

                with contextlib.ExitStack() as esa:
                    if not want("b3"):
                        raise StopBuild
                    SBa = lambda n, s, d: esa.enter_context(nc.sbuf_tensor(n, s, d))
                    PSa = lambda n, s, d: esa.enter_context(nc.psum_tensor(n, s, d))
                    bias_b = [SBa(f"bias{i}", [128, T], F32) for i in range(2)]
                    s_sb = [SBa(f"s_sb{i}", [128, T], F32) for i in range(2)]
                    p_sb = [SBa(f"p_sb{i}", [128, T], BF16) for i in range(2)]
                    pt_sb = [SBa(f"pt_sb{i}", [128, T], BF16) for i in range(2)]
                    nm = [SBa(f"nm{i}", [128, 1], F32) for i in range(2)]
                    sm = [SBa(f"sm{i}", [128, 1], F32) for i in range(2)]
                    rc = [SBa(f"rc{i}", [128, 1], F32) for i in range(2)]
                    stage = [SBa(f"stage{i}", [128, 128], BF16) for i in range(2)]
                    s_ps = [PSa(f"s_ps{i}", [128, 1024], F32) for i in range(2)]
                    mx2 = [SBa(f"mx2_{i}", [128, 2], F32) for i in range(2)]
                    tp3 = [PSa(f"tp3_{i}", [128, 1024], BF16) for i in range(2)]
                    av = [PSa(f"av{i}", [128, 128], F32) for i in range(2)]

                    NIT = H * NT

                    def nparts(it):
                        return 1 if (it % 16) < 8 else 2

                    def parts_upto(it):
                        return sum(nparts(i) for i in range(it + 1))

                    def batches_upto(it):
                        n = 0
                        for i in range(it + 1):
                            n += (i % 16 + 8) // 8
                        return n

                    def emit_tp_av_pe(t, pit):
                        pqt = pit % 16
                        t.wait_ge(S["ACT_ex"], pit + 1)
                        nb = (pqt + 8) // 8
                        b0 = batches_upto(pit - 1)
                        for b in range(nb):
                            gb = b0 + b
                            if gb >= 2:
                                t.wait_ge(S["DVE_te"], gb - 1)
                            for j in range(min(8, pqt + 1 - b * 8)):
                                ub = b * 8 + j
                                ins = t.transpose(
                                    tp3[gb % 2][:, j * 128:(j + 1) * 128],
                                    p_sb[pit % 2][:, ub * 128:(ub + 1) * 128],
                                    ident_b[:])
                            ins.then_inc(S["PE_tp"], 1)
                        t.wait_ge(S["DVE_te"], b0 + nb)
                        if pit >= 2:
                            t.wait_ge(S["ACT_ae"], pit - 1)
                        pg = (pit // 16) % G
                        for ub in range(pqt + 1):
                            ins = t.matmul(
                                av[pit % 2][:],
                                pt_sb[pit % 2][:, ub * 128:(ub + 1) * 128],
                                vt[:, ub * 512 + pg * 128:ub * 512 + (pg + 1) * 128],
                                start=(ub == 0), stop=(ub == pqt))
                        ins.then_inc(S["PE_av"], 1)

                    with nc.Block() as blk:
                        @blk.sync
                        def _(sy):
                            for h in range(2):
                                sy.dma_start(bias_b[h][:],
                                             bass.AP(alibi_d, h * T, [[0, 128], [1, T]])
                                             ).then_inc(S["SY_bias"], 16)
                            for h in range(H):
                                for qt in range(NT):
                                    it = h * 16 + qt
                                    sy.wait_ge(S["ACT_ae"], it + 1)
                                    sy.dma_start(
                                        ao_s.ap()[qt * 128:(qt + 1) * 128,
                                                  h * 128:(h + 1) * 128],
                                        stage[it % 2][:]).then_inc(S["SY_ao"], 16)
                                if h + 2 < H:
                                    sy.wait_ge(S["DVE_ev"], (h + 1) * 24)
                                    sy.dma_start(
                                        bias_b[h % 2][:],
                                        bass.AP(alibi_d, (h + 2) * T, [[0, 128], [1, T]])
                                        ).then_inc(S["SY_bias"], 16)

                        @blk.tensor
                        def _(t):
                            for h in range(H):
                                for qt in range(NT):
                                    it = h * 16 + qt
                                    L = (qt + 1) * 128
                                    gs = h % G
                                    gp0 = parts_upto(it - 1)
                                    np_ = nparts(it)
                                    for p in range(np_):
                                        gp = gp0 + p
                                        pa = p * 1024
                                        pb = min(L, (p + 1) * 1024)
                                        if gp >= 2:
                                            t.wait_ge(S["DVE_ev"], gp - 1)
                                        last = p == np_ - 1
                                        u0 = pa
                                        while u0 < pb:
                                            ul = min(512, pb - u0)
                                            ins = t.matmul(
                                                s_ps[gp % 2][:, u0 - pa:u0 - pa + ul],
                                                qf[:, h * T + qt * 128:
                                                   h * T + (qt + 1) * 128],
                                                kf[:, gs * T + u0:gs * T + u0 + ul],
                                                start=True, stop=not last)
                                            u0 += ul
                                        if last:
                                            ins = t.matmul(
                                                s_ps[gp % 2][:, qt * 128 - pa:
                                                             (qt + 1) * 128 - pa],
                                                ident_b[:], mask_sb[:],
                                                start=False, stop=True)
                                        ins.then_inc(S["PE_s"], 1)
                                    if it >= 1:
                                        emit_tp_av_pe(t, it - 1)
                            emit_tp_av_pe(t, NIT - 1)

                        @blk.vector
                        def _(v):
                            def do_recip(pit):
                                v.wait_ge(S["ACT_ex"], pit + 1)
                                v.reciprocal(rc[pit % 2][:], sm[pit % 2][:]
                                             ).then_inc(S["DVE_rc"], 1)
                            for h in range(H):
                                for qt in range(NT):
                                    it = h * 16 + qt
                                    L = (qt + 1) * 128
                                    v.wait_ge(S["SY_bias"], 16 * (h + 1))
                                    if it >= 2:
                                        v.wait_ge(S["ACT_ex"], it - 1)
                                    gp0 = parts_upto(it - 1)
                                    np_ = nparts(it)
                                    for p in range(np_):
                                        gp = gp0 + p
                                        pa = p * 1024
                                        pb = min(L, (p + 1) * 1024)
                                        v.wait_ge(S["PE_s"], gp + 1)
                                        v.tensor_add(s_sb[it % 2][:, pa:pb],
                                                     s_ps[gp % 2][:, 0:pb - pa],
                                                     bias_b[h % 2][:, pa:pb]
                                                     ).then_inc(S["DVE_ev"], 1)
                                        v.reduce_max(mx2[it % 2][:, p:p + 1],
                                                     s_sb[it % 2][:, pa:pb],
                                                     axis=AX.X
                                                     ).then_inc(S["DVE_mxp"], 1)
                                    v.wait_ge(S["DVE_mxp"], gp0 + np_)
                                    v.reduce_max(nm[it % 2][:],
                                                 mx2[it % 2][:, 0:np_],
                                                 axis=AX.X, negate=True
                                                 ).then_inc(S["DVE_mx"], 1)
                                    if it >= 1:
                                        do_recip(it - 1)
                            do_recip(NIT - 1)

                        @blk.scalar
                        def _(sc):
                            def tpev_act(pit):
                                pqt = pit % 16
                                nb = (pqt + 8) // 8
                                b0 = batches_upto(pit - 1)
                                if pit >= 2:
                                    sc.wait_ge(S["PE_av"], pit - 1)
                                for b in range(nb):
                                    gb = b0 + b
                                    sc.wait_ge(S["PE_tp"], gb + 1)
                                    nj = min(8, pqt + 1 - b * 8)
                                    sc.copy(
                                        pt_sb[pit % 2][:, b * 1024:b * 1024 + nj * 128],
                                        tp3[gb % 2][:, 0:nj * 128]
                                        ).then_inc(S["DVE_te"], 1)
                                sc.wait_ge(S["PE_av"], pit + 1)
                                sc.wait_ge(S["DVE_rc"], pit + 1)
                                if pit >= 2:
                                    sc.wait_ge(S["SY_ao"], 16 * (pit - 1))
                                sc.mul(stage[pit % 2][:], av[pit % 2][:],
                                       rc[pit % 2][:]).then_inc(S["ACT_ae"], 1)
                            for h in range(H):
                                for qt in range(NT):
                                    it = h * 16 + qt
                                    L = (qt + 1) * 128
                                    sc.wait_ge(S["DVE_mx"], it + 1)
                                    if it >= 2:
                                        sc.wait_ge(S["PE_tp"], batches_upto(it - 2))
                                        sc.wait_ge(S["DVE_rc"], it - 1)
                                    sc.activation(p_sb[it % 2][:, 0:L],
                                                  s_sb[it % 2][:, 0:L],
                                                  AF.Exp, bias=nm[it % 2][:], scale=1.0,
                                                  accum_out=sm[it % 2][:]
                                                  ).then_inc(S["ACT_ex"], 1)
                                    if it >= 1:
                                        tpev_act(it - 1)
                            tpev_act(NIT - 1)

            if "dbg_ao" in dbg:
                with nc.Block() as blk:
                    @blk.gpsimd
                    def _(g):
                        g.wait_ge(S["SY_ao"], 16 * 256)
                        g.dma_start(dbg["dbg_ao"].ap(), ao_s.ap()
                                    ).then_inc(S["SY_dbg"], 16)

            with contextlib.ExitStack() as esw:
                if not want("b4"):
                    raise StopBuild
                SBw = lambda n, s, d: esw.enter_context(nc.sbuf_tensor(n, s, d))
                PSw = lambda n, s, d: esw.enter_context(nc.psum_tensor(n, s, d))
                wo_sb = SBw("wo_sb", [128, KC * C], BF16)
                aot = [SBw(f"aot{i}", [128, C], BF16) for i in range(2)]
                aoT = [SBw(f"aoT{i}", [128, C], BF16) for i in range(2)]
                xres = [SBw(f"xres{i}", [128, C], F32) for i in range(2)]
                r1 = [SBw(f"r1_{i}", [128, C], F32) for i in range(2)]
                y1 = [SBw(f"y1_{i}", [128, C], F32) for i in range(2)]
                stats = SBw("stats", [128, 24], F32)
                aggr = SBw("aggr", [128, 2], F32)
                vecol = SBw("vecol", [128, 1], F32)
                lncol = SBw("lncol", [128, 1], F32)
                rstd = SBw("rstd", [128, 1], F32)
                wo_ps = PSw("wo_ps", [128, C], F32)
                tp4 = [PSw(f"tp4_{i}", [128, 1024], BF16) for i in range(2)]

                with nc.Block() as blk:
                    @blk.gpsimd
                    def _(g):
                        for Q in range(4):
                            src = wo_d.ap()[:, Q * 512:(Q + 1) * 512]
                            src = src.rearrange("(kc p) f -> p kc f", p=128)
                            dst = wo_sb[:].rearrange("p (kc f) -> p kc f", kc=KC)
                            g.dma_start(dst[:, :, Q * 512:(Q + 1) * 512], src
                                        ).then_inc(S["GP_w"], 16)
                        for qt in range(NT):
                            g.wait_ge(S["DVE_y1"], qt + 1)
                            g.dma_start(y1b_s.ap()[qt * 128:(qt + 1) * 128, :],
                                        y1[qt % 2][:]).then_inc(S["GP_y1b"], 16)

                    @blk.sync
                    def _(sy):
                        sy.wait_ge(S["SY_ao"], 16 * 256)
                        for qt in range(NT):
                            if qt >= 2:
                                sy.wait_ge(S["PE_t5"], 2 * qt - 2)
                            sy.dma_start(aot[qt % 2][:],
                                         ao_s.ap()[qt * 128:(qt + 1) * 128, :]
                                         ).then_inc(S["SY_aot"], 16)
                            if qt >= 2:
                                sy.wait_ge(S["DVE_r1"], qt - 1)
                            sy.dma_start(xres[qt % 2][:],
                                         x_d.ap()[qt * 128:(qt + 1) * 128, :]
                                         ).then_inc(S["SY_xres"], 16)
                            if qt >= 1:
                                sy.wait_ge(S["DVE_y1"], qt)
                                sy.dma_start(y1f_s.ap()[(qt - 1) * 128:qt * 128, :],
                                             y1[(qt - 1) % 2][:]
                                             ).then_inc(S["SY_y1f"], 16)
                        sy.wait_ge(S["DVE_y1"], NT)
                        sy.dma_start(y1f_s.ap()[(NT - 1) * 128:NT * 128, :],
                                     y1[(NT - 1) % 2][:]).then_inc(S["SY_y1f"], 16)

                    @blk.tensor
                    def _(t):
                        t.wait_ge(S["GP_w"], 16 * 16)
                        for qt in range(NT):
                            t.wait_ge(S["SY_aot"], 16 * (qt + 1))
                            for b in range(2):
                                gb = qt * 2 + b
                                if gb >= 2:
                                    t.wait_ge(S["DVE_te5"], gb - 1)
                                for j in range(8):
                                    c = b * 8 + j
                                    ins = t.transpose(
                                        tp4[gb % 2][:, j * 128:(j + 1) * 128],
                                        aot[qt % 2][:, c * 128:(c + 1) * 128],
                                        ident_b[:])
                                ins.then_inc(S["PE_t5"], 1)
                            t.wait_ge(S["DVE_te5"], qt * 2 + 2)
                            if qt >= 1:
                                t.wait_ge(S["DVE_r1"], qt)
                            for sp in range(4):
                                for c in range(KC):
                                    t.matmul(wo_ps[:, sp * 512:(sp + 1) * 512],
                                             aoT[qt % 2][:, c * 128:(c + 1) * 128],
                                             wo_sb[:, c * C + sp * 512:
                                                   c * C + (sp + 1) * 512],
                                             start=(c == 0), stop=False)
                                t.matmul(wo_ps[:, sp * 512:(sp + 1) * 512], ones_b[:],
                                         bo_hi[:, sp * 512:(sp + 1) * 512],
                                         start=False, stop=False)
                                ins = t.matmul(wo_ps[:, sp * 512:(sp + 1) * 512],
                                               ones_b[:],
                                               bo_lo[:, sp * 512:(sp + 1) * 512],
                                               start=False, stop=True)
                            ins.then_inc(S["PE_wo"], 1)

                    @blk.vector
                    def _(v):
                        for qt in range(NT):
                            for b in range(2):
                                gb = qt * 2 + b
                                v.wait_ge(S["PE_t5"], gb + 1)
                                if qt >= 2:
                                    v.wait_ge(S["PE_wo"], qt - 1)
                                v.tensor_copy(aoT[qt % 2][:, b * 1024:(b + 1) * 1024],
                                              tp4[gb % 2][:]).then_inc(S["DVE_te5"], 1)
                            v.wait_ge(S["PE_wo"], qt + 1)
                            v.wait_ge(S["SY_xres"], 16 * (qt + 1))
                            if qt >= 2:
                                v.wait_ge(S["GP_y1b"], 16 * (qt - 1))
                                v.wait_ge(S["SY_y1f"], 16 * (qt - 1))
                            v.tensor_add(r1[qt % 2][:], wo_ps[:], xres[qt % 2][:]
                                         ).then_inc(S["DVE_r1"], 1)
                            for i in range(4):
                                v.bn_stats(stats[:, 6 * i:6 * (i + 1)],
                                           r1[qt % 2][:, 512 * i:512 * (i + 1)]
                                           ).then_inc(S["DVE_bn"], 1)
                            v.wait_ge(S["DVE_bn"], 6 * qt + 4)
                            v.bn_aggr(aggr[:], stats[:]).then_inc(S["DVE_bn"], 1)
                            v.wait_ge(S["DVE_bn"], 6 * qt + 5)
                            v.tensor_scalar_add(vecol[:], aggr[:, 1:2], EPS
                                                ).then_inc(S["DVE_bn"], 1)
                            v.wait_ge(S["ACT_rs"], 2 * qt + 2)
                            v.tensor_scalar(y1[qt % 2][:], r1[qt % 2][:],
                                            aggr[:, 0:1], rstd[:],
                                            ALU.subtract, ALU.mult
                                            ).then_inc(S["DVE_y1"], 1)

                    @blk.scalar
                    def _(sc):
                        for qt in range(NT):
                            sc.wait_ge(S["DVE_bn"], 6 * qt + 6)
                            sc.activation(lncol[:], vecol[:], AF.Ln
                                          ).then_inc(S["ACT_rs"], 1)
                            sc.wait_ge(S["ACT_rs"], 2 * qt + 1)
                            sc.activation(rstd[:], lncol[:], AF.Exp, scale=-0.5
                                          ).then_inc(S["ACT_rs"], 1)

            if "dbg_y1" in dbg:
                with nc.Block() as blk:
                    @blk.sync
                    def _(sy):
                        sy.wait_ge(S["SY_y1f"], 16 * NT)
                        sy.dma_start(dbg["dbg_y1"].ap(), y1f_s.ap()
                                     ).then_inc(S["SY_dbg"], 16)

            with contextlib.ExitStack() as esy:
                y1T = esy.enter_context(nc.sbuf_tensor("y1T", [128, KC * T], BF16))

                with contextlib.ExitStack() as esr:
                    if not want("b6"):
                        raise StopBuild
                    SBr = lambda n, s, d: esr.enter_context(nc.sbuf_tensor(n, s, d))
                    PSr = lambda n, s, d: esr.enter_context(nc.psum_tensor(n, s, d))
                    y1b = [SBr(f"y1b{i}", [128, C], BF16) for i in range(2)]
                    lg = SBr("lg", [128, E], F32)
                    top8 = SBr("top8", [128, E], F32)
                    idx_u = SBr("idx_u", [128, E], U32)
                    idx_f = SBr("idx_f", [128, 2], F32)
                    nm1 = SBr("rt_nm1", [128, 1], F32)
                    e2c = SBr("e2c", [128, 1], F32)
                    dnc = SBr("dnc", [128, 1], F32)
                    w1c = SBr("w1c", [128, 1], F32)
                    w2c = SBr("w2c", [128, 1], F32)
                    oh1 = SBr("oh1", [128, E], F32)
                    oh2 = SBr("oh2", [128, E], F32)
                    combt = SBr("combt", [128, E], F32)
                    comb_sb = SBr("comb_sb", [8, T], F32)
                    tp5 = [PSr(f"tp5_{i}", [128, 1024], BF16) for i in range(2)]
                    gate_ps = PSr("gate_ps", [128, E], F32)
                    ct_ps = PSr("ct_ps", [128, 128], F32)

                    with nc.Block() as blk:
                        @blk.sync
                        def _(sy):
                            sy.wait_ge(S["GP_y1b"], 16 * NT)
                            for tt in range(NT):
                                if tt >= 2:
                                    sy.wait_ge(S["PE_gate"], tt - 1)
                                sy.dma_start(y1b[tt % 2][:],
                                             y1b_s.ap()[tt * 128:(tt + 1) * 128, :]
                                             ).then_inc(S["SY_y1rd"], 16)
                            sy.wait_ge(S["DVE_cte"], NT)
                            sy.dma_start(comb_s.ap(), comb_sb[:]
                                         ).then_inc(S["SY_comb"], 16)

                        @blk.tensor
                        def _(t):
                            for tt in range(NT):
                                t.wait_ge(S["SY_y1rd"], 16 * (tt + 1))
                                for b in range(2):
                                    gb = tt * 2 + b
                                    if gb >= 2:
                                        t.wait_ge(S["DVE_te5"], 32 + gb - 1)
                                    for j in range(8):
                                        c = b * 8 + j
                                        ins = t.transpose(
                                            tp5[gb % 2][:, j * 128:(j + 1) * 128],
                                            y1b[tt % 2][:, c * 128:(c + 1) * 128],
                                            ident_b[:])
                                    ins.then_inc(S["PE_t5"], 1)
                                t.wait_ge(S["DVE_te5"], 32 + tt * 2 + 2)
                                if tt >= 1:
                                    t.wait_ge(S["DVE_rt"], (tt - 1) * 11 + 1)
                                for kc in range(KC):
                                    ins = t.matmul(
                                        gate_ps[:],
                                        y1T[:, kc * T + tt * 128:
                                           kc * T + (tt + 1) * 128],
                                        gate_sb[:, kc * E:(kc + 1) * E],
                                        start=(kc == 0), stop=(kc == KC - 1))
                                ins.then_inc(S["PE_gate"], 1)
                                t.wait_ge(S["DVE_rt"], (tt + 1) * 11)
                                if tt >= 1:
                                    t.wait_ge(S["DVE_cte"], tt)
                                t.transpose(ct_ps[0:E, :], combt[:], ident_f[:]
                                            ).then_inc(S["PE_ct"], 1)

                        @blk.vector
                        def _(v):
                            y1Tv = y1T[:].rearrange("p (kc t) -> p kc t", kc=KC)
                            for tt in range(NT):
                                for b in range(2):
                                    gb = tt * 2 + b
                                    v.wait_ge(S["PE_t5"], 32 + gb + 1)
                                    src = tp5[gb % 2][:].rearrange(
                                        "p (j t) -> p j t", j=8)
                                    dst = y1Tv[:, b * 8:b * 8 + 8,
                                               tt * 128:(tt + 1) * 128]
                                    v.tensor_copy(dst, src).then_inc(S["DVE_te5"], 1)
                                v.wait_ge(S["PE_gate"], tt + 1)
                                if tt >= 1:
                                    v.wait_ge(S["PE_ct"], tt)
                                v.tensor_copy(lg[:], gate_ps[:]).then_inc(S["DVE_rt"], 1)
                                v.wait_ge(S["DVE_rt"], tt * 11 + 1)
                                v.max(top8[:], lg[:]).then_inc(S["DVE_rt"], 1)
                                v.wait_ge(S["DVE_rt"], tt * 11 + 2)
                                v.max_index(idx_u[:], top8[:], lg[:]
                                            ).then_inc(S["DVE_rt"], 1)
                                v.wait_ge(S["DVE_rt"], tt * 11 + 3)
                                v.tensor_copy(idx_f[:], idx_u[:, 0:2]
                                              ).then_inc(S["DVE_rt"], 1)
                                v.tensor_scalar_mul(nm1[:], top8[:, 0:1], -1.0
                                                    ).then_inc(S["DVE_rt"], 1)
                                v.wait_ge(S["ACT_rt"], tt + 1)
                                v.tensor_scalar_add(dnc[:], e2c[:], 1.0
                                                    ).then_inc(S["DVE_rt"], 1)
                                v.wait_ge(S["DVE_rt"], tt * 11 + 6)
                                v.reciprocal(w1c[:], dnc[:]).then_inc(S["DVE_rt"], 1)
                                v.wait_ge(S["DVE_rt"], tt * 11 + 7)
                                v.tensor_mul(w2c[:], e2c[:], w1c[:]
                                             ).then_inc(S["DVE_rt"], 1)
                                v.wait_ge(S["DVE_rt"], tt * 11 + 8)
                                v.tensor_scalar(oh1[:], iota8[:], idx_f[:, 0:1], None,
                                                ALU.is_equal)
                                v.tensor_scalar(oh2[:], iota8[:], idx_f[:, 1:2], None,
                                                ALU.is_equal).then_inc(S["DVE_rt"], 1)
                                v.wait_ge(S["DVE_rt"], tt * 11 + 9)
                                v.tensor_scalar_mul(oh1[:], oh1[:], w1c[:])
                                v.tensor_scalar_mul(oh2[:], oh2[:], w2c[:]
                                                    ).then_inc(S["DVE_rt"], 1)
                                v.wait_ge(S["DVE_rt"], tt * 11 + 10)
                                v.tensor_add(combt[:], oh1[:], oh2[:]
                                             ).then_inc(S["DVE_rt"], 1)
                                v.wait_ge(S["PE_ct"], tt + 1)
                                v.tensor_copy(comb_sb[0:E, tt * 128:(tt + 1) * 128],
                                              ct_ps[0:E, :]).then_inc(S["DVE_cte"], 1)

                        @blk.scalar
                        def _(sc):
                            for tt in range(NT):
                                sc.wait_ge(S["DVE_rt"], tt * 11 + 5)
                                sc.activation(e2c[:], top8[:, 1:2], AF.Exp,
                                              bias=nm1[:], scale=1.0
                                              ).then_inc(S["ACT_rt"], 1)

                if "dbg_comb" in dbg:
                    with nc.Block() as blk:
                        @blk.sync
                        def _(sy):
                            sy.wait_ge(S["SY_comb"], 16)
                            sy.dma_start(dbg["dbg_comb"].ap(), comb_s.ap()
                                         ).then_inc(S["SY_dbg"], 16)

                with contextlib.ExitStack() as esf:
                    if not want("b7"):
                        raise StopBuild
                    SBf = lambda n, s, d: esf.enter_context(nc.sbuf_tensor(n, s, d))
                    PSf = lambda n, s, d: esf.enter_context(nc.psum_tensor(n, s, d))
                    cb = [SBf(f"cb{i}", [128, T], F32) for i in range(2)]
                    wgb = [SBf(f"wgb{i}", [128, KC * 128], BF16) for i in range(2)]
                    wub = [SBf(f"wub{i}", [128, KC * 128], BF16) for i in range(2)]
                    sg = [SBf(f"sg{i}", [128, 1024], BF16) for i in range(2)]
                    uc = SBf("uc", [128, 1024], F32)
                    ab = [SBf(f"ab{i}", [128, 1024], BF16) for i in range(2)]
                    wdnb = [SBf(f"wdnb{i}", [128, C], BF16) for i in range(2)]
                    g_ps = [PSf(f"g_ps{i}", [128, 1024], F32) for i in range(2)]
                    u_ps = [PSf(f"u_ps{i}", [128, 1024], F32) for i in range(2)]

                    with nc.Block() as blk:
                        @blk.sync
                        def _(sy):
                            sy.wait_ge(S["SY_comb"], 16)
                            sy.dma_start(cb[0][:],
                                         bass.AP(comb_s, 0, [[0, 128], [1, T]])
                                         ).then_inc(S["SY_cb"], 16)
                            sy.dma_start(cb[1][:],
                                         bass.AP(comb_s, T, [[0, 128], [1, T]])
                                         ).then_inc(S["SY_cb"], 16)
                            for e in range(E):
                                if 2 <= e:
                                    pass
                                for mc in range(MC):
                                    j = e * MC + mc
                                    sy.wait_ge(S["GP_wdn"], 16 * (j + 1))
                                    sy.dma_start(wdn_s.ap()[e, mc], wdnb[j % 2][:]
                                                 ).then_inc(S["SY_wdno"], 16)
                                    for th in range(2):
                                        it = j * 2 + th
                                        sy.wait_ge(S["DVE_a"], it + 1)
                                        dst = a_s.ap()[th * 8:(th + 1) * 8, :,
                                                       e:e + 1, mc:mc + 1, :]
                                        dst = dst.rearrange(
                                            "tt f e m t -> f tt (e m t)")
                                        src = ab[it % 2][:].rearrange(
                                            "f (tt t) -> f tt t", tt=8)
                                        sy.dma_start(dst, src).then_inc(S["SY_aout"], 16)
                                if e + 2 < E:
                                    sy.wait_ge(S["DVE_a"], (e + 1) * 2 * MC)
                                    sy.dma_start(cb[e % 2][:],
                                                 bass.AP(comb_s, (e + 2) * T,
                                                         [[0, 128], [1, T]])
                                                 ).then_inc(S["SY_cb"], 16)

                        @blk.gpsimd
                        def _(g):
                            for e in range(E):
                                for mc in range(MC):
                                    j = e * MC + mc
                                    if j >= 2:
                                        g.wait_ge(S["PE_f1"], 2 * (j - 1))
                                    src = wg_d.ap()[e, :, mc * 128:(mc + 1) * 128]
                                    src = src.rearrange("(kc p) m -> p kc m", p=128)
                                    dst = wgb[j % 2][:].rearrange(
                                        "p (kc m) -> p kc m", kc=KC)
                                    g.dma_start(dst, src).then_inc(S["GP_gw"], 16)
                                    src = wu_d.ap()[e, :, mc * 128:(mc + 1) * 128]
                                    src = src.rearrange("(kc p) m -> p kc m", p=128)
                                    dst = wub[j % 2][:].rearrange(
                                        "p (kc m) -> p kc m", kc=KC)
                                    g.dma_start(dst, src).then_inc(S["GP_gw"], 16)
                                    if j >= 2:
                                        g.wait_ge(S["SY_wdno"], 16 * (j - 1))
                                    g.dma_start(wdnb[j % 2][:],
                                                wd_d.ap()[e, mc * 128:(mc + 1) * 128, :]
                                                ).then_inc(S["GP_wdn"], 16)

                        @blk.tensor
                        def _(t):
                            for e in range(E):
                                for mc in range(MC):
                                    j = e * MC + mc
                                    t.wait_ge(S["GP_gw"], 32 * (j + 1))
                                    for th in range(2):
                                        it = j * 2 + th
                                        if it >= 2:
                                            t.wait_ge(S["ACT_si"], it - 1)
                                            t.wait_ge(S["DVE_uc"], it - 1)
                                        for sp in range(2):
                                            t0 = th * 1024 + sp * 512
                                            for kc in range(KC):
                                                t.matmul(
                                                    g_ps[it % 2][:, sp * 512:(sp + 1) * 512],
                                                    wgb[j % 2][:, kc * 128:(kc + 1) * 128],
                                                    y1T[:, kc * T + t0:kc * T + t0 + 512],
                                                    start=(kc == 0), stop=(kc == KC - 1))
                                            for kc in range(KC):
                                                ins = t.matmul(
                                                    u_ps[it % 2][:, sp * 512:(sp + 1) * 512],
                                                    wub[j % 2][:, kc * 128:(kc + 1) * 128],
                                                    y1T[:, kc * T + t0:kc * T + t0 + 512],
                                                    start=(kc == 0), stop=(kc == KC - 1))
                                        ins.then_inc(S["PE_f1"], 1)

                        @blk.scalar
                        def _(sc):
                            for e in range(E):
                                for mc in range(MC):
                                    j = e * MC + mc
                                    for th in range(2):
                                        it = j * 2 + th
                                        sc.wait_ge(S["PE_f1"], it + 1)
                                        if it >= 2:
                                            sc.wait_ge(S["DVE_a"], it - 1)
                                        sc.activation(sg[it % 2][:], g_ps[it % 2][:],
                                                      AF.Silu).then_inc(S["ACT_si"], 1)

                        @blk.vector
                        def _(v):
                            for e in range(E):
                                for mc in range(MC):
                                    j = e * MC + mc
                                    for th in range(2):
                                        it = j * 2 + th
                                        v.wait_ge(S["PE_f1"], it + 1)
                                        v.wait_ge(S["SY_cb"], 16 * (e + 1))
                                        v.tensor_mul(uc[:], u_ps[it % 2][:],
                                                     cb[e % 2][:, th * 1024:(th + 1) * 1024]
                                                     ).then_inc(S["DVE_uc"], 1)
                                        v.wait_ge(S["ACT_si"], it + 1)
                                        if it >= 2:
                                            v.wait_ge(S["SY_aout"], 16 * (it - 1))
                                        v.tensor_mul(ab[it % 2][:], uc[:], sg[it % 2][:]
                                                     ).then_inc(S["DVE_a"], 1)

            with contextlib.ExitStack() as es8:
                if not want("b8"):
                    raise StopBuild
                SB8 = lambda n, s, d: es8.enter_context(nc.sbuf_tensor(n, s, d))
                PS8 = lambda n, s, d: es8.enter_context(nc.psum_tensor(n, s, d))
                NE = E * MC
                a_sb = SB8("a_sb", [128, 4 * NE * 128], BF16)
                wdn2 = [SB8(f"wdn2_{i}", [128, 2048], BF16) for i in range(4)]
                y1res = [SB8(f"y1res{i}", [128, C], F32) for i in range(4)]
                r2 = [SB8(f"r2_{i}", [128, C], F32) for i in range(4)]
                outb = [SB8(f"outb{i}", [128, C], F32) for i in range(2)]
                stats2 = SB8("stats2", [128, 24], F32)
                aggr2 = SB8("aggr2", [128, 2], F32)
                vecol2 = SB8("vecol2", [128, 1], F32)
                lncol2 = SB8("lncol2", [128, 1], F32)
                rstd2 = SB8("rstd2", [128, 1], F32)
                ffn_ps = [PS8(f"ffn_ps{i}", [128, 1024], F32) for i in range(4)]

                with nc.Block() as blk:
                    @blk.sync
                    def _(sy):
                        sy.wait_ge(S["SY_aout"], 16 * 2 * NE)
                        sy.wait_ge(S["SY_wdno"], 16 * NE)
                        for tb in range(4):
                            if tb >= 1:
                                sy.wait_ge(S["PE_f2"], tb * NE)
                            for ttl in range(4):
                                tt = tb * 4 + ttl
                                dst = a_sb[:, ttl * NE * 128:(ttl + 1) * NE * 128]
                                src = a_s.ap()[tt:tt + 1].rearrange(
                                    "z f e m t -> (z f) (e m t)")
                                sy.dma_start(dst, src).then_inc(S["SY_ain"], 16)
                                if tb >= 1:
                                    sy.wait_ge(S["DVE_f2"], (tb - 1) * 8 + 4 + ttl + 1)
                                sy.dma_start(y1res[ttl][:],
                                             y1f_s.ap()[tt * 128:(tt + 1) * 128, :]
                                             ).then_inc(S["SY_y1res"], 16)
                            wdn_flat = wdn_s.ap().rearrange("e m f c -> (e m) f c")
                            for hh in range(2):
                                for pj in range(NE // 2):
                                    gp = (tb * 2 + hh) * (NE // 2) + pj
                                    if gp % 2:
                                        continue
                                    if gp >= 4:
                                        sy.wait_ge(S["PE_f2"], gp - 3)
                                    src = wdn_flat[2 * pj:2 * pj + 2, :,
                                                   hh * 1024:(hh + 1) * 1024]
                                    src = src.rearrange("m f c -> f m c")
                                    dst = wdn2[gp % 4][:].rearrange(
                                        "f (m c) -> f m c", m=2)
                                    sy.dma_start(dst, src).then_inc(S["SY_wdn2"], 16)
                            for ttl in range(4):
                                tt = tb * 4 + ttl
                                sy.wait_ge(S["DVE_o"], tt + 1)
                                sy.dma_start(out_d.ap()[tt * 128:(tt + 1) * 128, :],
                                             outb[ttl % 2][:]).then_inc(S["SY_out"], 16)
                        sy.wait_ge(S["SY_out"], 16 * NT)

                    @blk.gpsimd
                    def _(g):
                        g.wait_ge(S["SY_wdno"], 16 * NE)
                        wdn_flat = wdn_s.ap().rearrange("e m f c -> (e m) f c")
                        for tb in range(4):
                            for hh in range(2):
                                for pj in range(NE // 2):
                                    gp = (tb * 2 + hh) * (NE // 2) + pj
                                    if gp % 2 == 0:
                                        continue
                                    if gp >= 4:
                                        g.wait_ge(S["PE_f2"], gp - 3)
                                    src = wdn_flat[2 * pj:2 * pj + 2, :,
                                                   hh * 1024:(hh + 1) * 1024]
                                    src = src.rearrange("m f c -> f m c")
                                    dst = wdn2[gp % 4][:].rearrange(
                                        "f (m c) -> f m c", m=2)
                                    g.dma_start(dst, src).then_inc(S["GP_wdn"], 16)

                    @blk.tensor
                    def _(t):
                        gwdn_base = 16 * NE
                        for tb in range(4):
                            t.wait_ge(S["SY_ain"], 16 * (tb * 4 + 4))
                            for hh in range(2):
                                rnd = tb * 2 + hh
                                if rnd >= 1:
                                    t.wait_ge(S["DVE_f2"], rnd * 4)
                                for pj in range(NE // 2):
                                    gp = rnd * (NE // 2) + pj
                                    if gp % 2 == 0:
                                        t.wait_ge(S["SY_wdn2"], 16 * (gp // 2 + 1))
                                    else:
                                        t.wait_ge(S["GP_wdn"],
                                                  gwdn_base + 16 * (gp // 2 + 1))
                                    for jin in range(2):
                                        jj = 2 * pj + jin
                                        for ttl in range(4):
                                            for sp in range(2):
                                                ins = t.matmul(
                                                    ffn_ps[ttl][:, sp * 512:(sp + 1) * 512],
                                                    a_sb[:, (ttl * NE + jj) * 128:
                                                         (ttl * NE + jj + 1) * 128],
                                                    wdn2[gp % 4][:, jin * 1024 + sp * 512:
                                                         jin * 1024 + (sp + 1) * 512],
                                                    start=(jj == 0), stop=(jj == NE - 1))
                                    ins.then_inc(S["PE_f2"], 1)

                    @blk.vector
                    def _(v):
                        for tb in range(4):
                            for hh in range(2):
                                rnd = tb * 2 + hh
                                v.wait_ge(S["PE_f2"], (rnd + 1) * (NE // 2))
                                for ttl in range(4):
                                    v.wait_ge(S["SY_y1res"], 16 * (tb * 4 + ttl + 1))
                                    v.tensor_add(r2[ttl][:, hh * 1024:(hh + 1) * 1024],
                                                 ffn_ps[ttl][:],
                                                 y1res[ttl][:, hh * 1024:(hh + 1) * 1024]
                                                 ).then_inc(S["DVE_f2"], 1)
                            for ttl in range(4):
                                tt = tb * 4 + ttl
                                v.wait_ge(S["DVE_f2"], tb * 8 + 4 + ttl + 1)
                                for i in range(4):
                                    v.bn_stats(stats2[:, 6 * i:6 * (i + 1)],
                                               r2[ttl][:, 512 * i:512 * (i + 1)]
                                               ).then_inc(S["DVE_ln2"], 1)
                                v.wait_ge(S["DVE_ln2"], 6 * tt + 4)
                                v.bn_aggr(aggr2[:], stats2[:]).then_inc(S["DVE_ln2"], 1)
                                v.wait_ge(S["DVE_ln2"], 6 * tt + 5)
                                v.tensor_scalar_add(vecol2[:], aggr2[:, 1:2], EPS
                                                    ).then_inc(S["DVE_ln2"], 1)
                                v.wait_ge(S["ACT_rs2"], 2 * tt + 2)
                                if tt >= 2:
                                    v.wait_ge(S["SY_out"], 16 * (tt - 1))
                                v.tensor_scalar(outb[ttl % 2][:], r2[ttl][:],
                                                aggr2[:, 0:1], rstd2[:],
                                                ALU.subtract, ALU.mult
                                                ).then_inc(S["DVE_o"], 1)

                    @blk.scalar
                    def _(sc):
                        for tb in range(4):
                            for ttl in range(4):
                                tt = tb * 4 + ttl
                                sc.wait_ge(S["DVE_ln2"], 6 * tt + 6)
                                sc.activation(lncol2[:], vecol2[:], AF.Ln
                                              ).then_inc(S["ACT_rs2"], 1)
                                sc.wait_ge(S["ACT_rs2"], 2 * tt + 1)
                                sc.activation(rstd2[:], lncol2[:], AF.Exp, scale=-0.5
                                              ).then_inc(S["ACT_rs2"], 1)

    except StopBuild:
        pass
    return nc


def make_consts():
    start = 2.0 ** (-2.0 ** (-(math.log2(H) - 3.0)))
    slopes = np.array([start ** (i + 1) for i in range(H)], dtype=np.float64)
    alibi = (slopes[:, None] * np.arange(T, dtype=np.float64)[None, :])
    import ml_dtypes
    mask = np.triu(np.full((128, 128), NEG, dtype=np.float32), k=1)
    return alibi.astype(np.float32), mask.astype(ml_dtypes.bfloat16)


_NC_CACHE = {}


def _get_nc(debug=(), stop="b8"):
    key = (tuple(sorted(debug)), stop)
    if key not in _NC_CACHE:
        _NC_CACHE[key] = build(key[0], stop)
    return _NC_CACHE[key]


def kernel(L0_attn_x, L0_attn_Wq, L0_attn_Wkv, L0_attn_Wo, L0_attn_bo,
           L0_ffn_gate_w, L0_ffn_w_gate, L0_ffn_w_up, L0_ffn_w_down,
           _debug=(), _stop="b8"):
    x = np.ascontiguousarray(np.asarray(L0_attn_x, dtype=np.float32))
    alibi, mask_b = make_consts()
    common = {
        "wq": np.ascontiguousarray(np.asarray(L0_attn_Wq, dtype=np.float32)),
        "wkv": np.ascontiguousarray(np.asarray(L0_attn_Wkv, dtype=np.float32)),
        "wo": np.ascontiguousarray(np.asarray(L0_attn_Wo, dtype=np.float32)),
        "bo": np.ascontiguousarray(
            np.asarray(L0_attn_bo, dtype=np.float32).reshape(1, C)),
        "gw": np.ascontiguousarray(np.asarray(L0_ffn_gate_w, dtype=np.float32)),
        "wg": np.ascontiguousarray(np.asarray(L0_ffn_w_gate, dtype=np.float32)),
        "wu": np.ascontiguousarray(np.asarray(L0_ffn_w_up, dtype=np.float32)),
        "wd": np.ascontiguousarray(np.asarray(L0_ffn_w_down, dtype=np.float32)),
        "alibi": alibi,
        "mask": mask_b,
    }
    in_maps = [dict(common, x=np.ascontiguousarray(x[i])) for i in range(8)]
    nc = _get_nc(_debug, _stop)
    res = run_bass_kernel_spmd(nc, in_maps, core_ids=list(range(8)))
    if _debug:
        return res
    return np.stack([r["out"] for r in res.results], axis=0)


# version 08
# speedup vs baseline: 1.1405x; 1.1405x over previous
import contextlib
import math
import numpy as np


class StopBuild(Exception):
    pass

import concourse.bass as bass
import concourse.mybir as mybir
from concourse.bass_utils import run_bass_kernel_spmd

F32 = mybir.dt.float32
BF16 = mybir.dt.bfloat16
U32 = mybir.dt.uint32
AF = mybir.ActivationFunctionType
ALU = mybir.AluOpType
AX = mybir.AxisListType

T = 2048
C = 2048
H = 16
G = 4
FF = 1408
E = 8
MC = 11
NT = 16
KC = 16
QSCALE = 1.0 / math.sqrt(128.0)
NEG = -30000.0
EPS = 1e-5


PHASES = ["b0", "b1", "b2", "b3", "b4", "b6", "b7", "b8"]


def build(debug=(), stop="b8"):
    want = lambda p: PHASES.index(p) <= PHASES.index(stop)
    nc = bass.Bass("TRN2", target_bir_lowering=False)

    x_d = nc.dram_tensor("x", [T, C], F32, kind="ExternalInput")
    wq_d = nc.dram_tensor("wq", [C, C], F32, kind="ExternalInput")
    wkv_d = nc.dram_tensor("wkv", [C, 1024], F32, kind="ExternalInput")
    wo_d = nc.dram_tensor("wo", [C, C], F32, kind="ExternalInput")
    bo_d = nc.dram_tensor("bo", [1, C], F32, kind="ExternalInput")
    gw_d = nc.dram_tensor("gw", [C, E], F32, kind="ExternalInput")
    wg_d = nc.dram_tensor("wg", [E, C, FF], F32, kind="ExternalInput")
    wu_d = nc.dram_tensor("wu", [E, C, FF], F32, kind="ExternalInput")
    wd_d = nc.dram_tensor("wd", [E, FF, C], F32, kind="ExternalInput")
    alibi_d = nc.dram_tensor("alibi", [H, T], F32, kind="ExternalInput")
    mask_d = nc.dram_tensor("mask", [128, 128], BF16, kind="ExternalInput")
    out_d = nc.dram_tensor("out", [T, C], F32, kind="ExternalOutput")
    ao_s = nc.dram_tensor("ao_s", [T, C], BF16, kind="Internal")
    y1f_s = nc.dram_tensor("y1f_s", [T, C], F32, kind="Internal")
    y1b_s = nc.dram_tensor("y1b_s", [T, C], BF16, kind="Internal")
    comb_s = nc.dram_tensor("comb_s", [E, T], F32, kind="Internal")
    a_s = nc.dram_tensor("a_s", [NT, 128, E, MC, 128], BF16, kind="Internal")
    wdn_s = nc.dram_tensor("wdn_s", [E, MC, 128, C], BF16, kind="Internal")

    dbg = {}
    for name, shape, dt in [
        ("dbg_q", [128, H * T], F32), ("dbg_k", [128, G * T], F32),
        ("dbg_vt", [128, NT * 512], F32), ("dbg_ao", [T, C], F32),
        ("dbg_y1", [T, C], F32), ("dbg_comb", [E, T], F32),
    ]:
        if name in debug:
            dbg[name] = nc.dram_tensor(name, shape, dt, kind="ExternalOutput")

    es = contextlib.ExitStack()
    SB = lambda n, s, d: es.enter_context(nc.sbuf_tensor(n, s, d))
    SEM = lambda n: es.enter_context(nc.semaphore(n))

    try:
        with es:
            ident_b = SB("ident_b", [128, 128], BF16)
            ident_f = SB("ident_f", [128, 128], F32)
            ones_b = SB("ones_b", [1, 128], BF16)
            zero_b = SB("zero_b", [128, 512], BF16)
            mask_sb = SB("mask_sb", [128, 128], BF16)
            bo_hi = SB("bo_hi", [1, C], BF16)
            bo_lo = SB("bo_lo", [1, C], BF16)
            gate_sb = SB("gate_sb", [128, KC * E], BF16)
            iota8 = SB("iota8", [128, E], F32)

            s_names = [
                "SY0", "GP0", "SY_x", "ACT_cast", "PE_t1", "DVE_te1",
                "GP_w", "PE_p2", "DVE_p2e", "PE_vt", "DVE_vte",
                "SY_bias", "PE_s", "DVE_ev", "DVE_mx", "DVE_mxp", "ACT_ex", "PE_tp",
                "DVE_te", "PE_av", "DVE_rc", "ACT_ae", "SY_ao",
                "SY_aot", "SY_xres", "PE_t5", "DVE_te5", "PE_wo", "DVE_r1",
                "DVE_bn", "ACT_rs", "DVE_y1", "SY_y1f", "GP_y1b",
                "SY_y1rd", "PE_gate", "DVE_rt", "ACT_rt", "PE_ct", "DVE_cte",
                "SY_comb",
                "GP_gw", "SY_cb", "PE_f1", "ACT_si", "DVE_uc", "DVE_a",
                "SY_aout", "GP_wdn", "SY_wdno",
                "SY_ain", "SY_wdn2", "SY_y1res", "PE_f2", "DVE_f2", "DVE_ln2",
                "ACT_rs2", "DVE_o", "SY_out", "SY_dbg",
            ]
            S = {n: SEM(n) for n in s_names}

            with contextlib.ExitStack() as e0:
                bo_sb = e0.enter_context(nc.sbuf_tensor("bo_sb", [1, C], F32))
                pcol = e0.enter_context(nc.sbuf_tensor("pcol", [128, 1], F32))
                iotaF = e0.enter_context(nc.sbuf_tensor("iotaF", [128, 128], F32))

                with nc.Block() as blk:
                    @blk.gpsimd
                    def _(g):
                        g.iota(pcol[:], [[0, 1]], channel_multiplier=1,
                               allow_small_or_imprecise_dtypes=True)
                        g.iota(iotaF[:], [[1, 128]], channel_multiplier=0,
                               allow_small_or_imprecise_dtypes=True)
                        g.iota(iota8[:], [[1, E]], channel_multiplier=0,
                               allow_small_or_imprecise_dtypes=True)
                        g.memset(ones_b[:], 1.0)
                        g.memset(zero_b[:], 0.0)
                        src = gw_d.ap().rearrange("(kc p) e -> p kc e", p=128)
                        dstg = gate_sb[:].rearrange("p (kc e) -> p kc e", kc=KC)
                        g.dma_start(dstg, src).then_inc(S["GP0"], 16)

                    @blk.sync
                    def _(sy):
                        sy.dma_start(mask_sb[:], mask_d.ap()).then_inc(S["SY0"], 16)
                        sy.dma_start(bo_sb[:], bo_d.ap()).then_inc(S["SY0"], 16)

                with nc.Block() as blk:
                    @blk.vector
                    def _(v):
                        v.wait_ge(S["SY0"], 32)
                        v.wait_ge(S["GP0"], 16)
                        v.tensor_scalar(ident_b[:], iotaF[:], pcol[:], None, ALU.is_equal)
                        v.tensor_scalar(ident_f[:], iotaF[:], pcol[:], None, ALU.is_equal)
                        v.tensor_copy(bo_hi[:], bo_sb[:])
                        v.tensor_sub(bo_lo[:], bo_sb[:], bo_hi[:])

            with contextlib.ExitStack() as esq:
                if not want("b1"):
                    raise StopBuild
                SBq = lambda n, s, d: esq.enter_context(nc.sbuf_tensor(n, s, d))
                PSq = lambda n, s, d: esq.enter_context(nc.psum_tensor(n, s, d))
                qf = SBq("qf", [128, H * T], BF16)
                kf = SBq("kf", [128, G * T], BF16)
                vt = SBq("vt", [128, NT * 512], BF16)

                with contextlib.ExitStack() as esx:
                    SBx = lambda n, s, d: esx.enter_context(nc.sbuf_tensor(n, s, d))
                    PSx = lambda n, s, d: esx.enter_context(nc.psum_tensor(n, s, d))
                    tp1 = [PSx(f"tp1_{i}", [128, 1024], BF16) for i in range(2)]
                    xT = SBx("xT", [128, KC * T], BF16)
                    vf = SBx("vf", [128, G * T], BF16)

                    with contextlib.ExitStack() as e1:
                        xbf = [e1.enter_context(nc.sbuf_tensor(f"xbf{i}", [128, C], BF16))
                               for i in range(2)]
                        with nc.Block() as blk:
                            @blk.gpsimd
                            def _(g):
                                for tt in range(NT):
                                    if tt >= 2:
                                        g.wait_ge(S["PE_t1"], 2 * tt - 2)
                                    g.dma_start(xbf[tt % 2][:],
                                                x_d.ap()[tt * 128:(tt + 1) * 128, :]
                                                ).then_inc(S["SY_x"], 16)

                            @blk.tensor
                            def _(t):
                                for tt in range(NT):
                                    t.wait_ge(S["SY_x"], 16 * (tt + 1))
                                    for b in range(2):
                                        gb = tt * 2 + b
                                        if gb >= 2:
                                            t.wait_ge(S["DVE_te1"], gb - 1)
                                        for j in range(8):
                                            c = b * 8 + j
                                            ins = t.transpose(
                                                tp1[gb % 2][:, j * 128:(j + 1) * 128],
                                                xbf[tt % 2][:, c * 128:(c + 1) * 128],
                                                ident_b[:])
                                        ins.then_inc(S["PE_t1"], 1)

                            @blk.vector
                            def _(v):
                                xTv = xT[:].rearrange("p (kc t) -> p kc t", kc=KC)
                                for tt in range(NT):
                                    for b in range(2):
                                        gb = tt * 2 + b
                                        v.wait_ge(S["PE_t1"], gb + 1)
                                        src = tp1[gb % 2][:].rearrange(
                                            "p (j t) -> p j t", j=8)
                                        dst = xTv[:, b * 8:b * 8 + 8,
                                                  tt * 128:(tt + 1) * 128]
                                        v.tensor_copy(dst, src).then_inc(S["DVE_te1"], 1)

                    with contextlib.ExitStack() as e2:
                        if not want("b2"):
                            raise StopBuild
                        wbuf = [e2.enter_context(
                            nc.sbuf_tensor(f"wbuf{i}", [128, KC * 256], BF16))
                            for i in range(2)]
                        pr = [e2.enter_context(
                            nc.psum_tensor(f"pr{i}", [128, 512], F32))
                            for i in range(2)]
                        with nc.Block() as blk:
                            @blk.gpsimd
                            def _(g):
                                for Q in range(12):
                                    if Q >= 2:
                                        g.wait_ge(S["PE_p2"], (Q - 1) * 8)
                                    if Q < 8:
                                        src = wq_d.ap()[:, Q * 256:(Q + 1) * 256]
                                    else:
                                        src = wkv_d.ap()[:, (Q - 8) * 256:(Q - 7) * 256]
                                    src = src.rearrange("(kc p) f -> p kc f", p=128)
                                    dst = wbuf[Q % 2][:].rearrange(
                                        "p (kc f) -> p kc f", kc=KC)
                                    g.dma_start(dst, src).then_inc(S["GP_w"], 16)

                            @blk.tensor
                            def _(t):
                                for Q in range(12):
                                    t.wait_ge(S["GP_w"], 16 * (Q + 1))
                                    for c in range(2):
                                        for ts in range(4):
                                            gi = Q * 8 + c * 4 + ts
                                            if gi >= 2:
                                                t.wait_ge(S["DVE_p2e"], gi - 1)
                                            for kc in range(KC):
                                                ins = t.matmul(
                                                    pr[gi % 2][:],
                                                    wbuf[Q % 2][:, kc * 256 + c * 128:
                                                                kc * 256 + (c + 1) * 128],
                                                    xT[:, kc * T + ts * 512:
                                                       kc * T + (ts + 1) * 512],
                                                    start=(kc == 0), stop=(kc == KC - 1))
                                            ins.then_inc(S["PE_p2"], 1)

                            @blk.vector
                            def _(v):
                                for Q in range(12):
                                    for c in range(2):
                                        for ts in range(4):
                                            gi = Q * 8 + c * 4 + ts
                                            v.wait_ge(S["PE_p2"], gi + 1)
                                            fo = Q * 2 + c
                                            if Q < 8:
                                                h = fo
                                                v.tensor_scalar_mul(
                                                    qf[:, h * T + ts * 512:
                                                       h * T + (ts + 1) * 512],
                                                    pr[gi % 2][:], QSCALE
                                                    ).then_inc(S["DVE_p2e"], 1)
                                            elif fo < 20:
                                                gg = fo - 16
                                                v.tensor_copy(
                                                    kf[:, gg * T + ts * 512:
                                                       gg * T + (ts + 1) * 512],
                                                    pr[gi % 2][:]
                                                    ).then_inc(S["DVE_p2e"], 1)
                                            else:
                                                gg = fo - 20
                                                v.tensor_copy(
                                                    vf[:, gg * T + ts * 512:
                                                       gg * T + (ts + 1) * 512],
                                                    pr[gi % 2][:]
                                                    ).then_inc(S["DVE_p2e"], 1)

                    with nc.Block() as blk:
                        @blk.tensor
                        def _(t):
                            for b in range(8):
                                if b >= 2:
                                    t.wait_ge(S["DVE_vte"], b - 1)
                                for j in range(8):
                                    i = b * 8 + j
                                    gg, ut = i // NT, i % NT
                                    ins = t.transpose(tp1[b % 2][:, j * 128:(j + 1) * 128],
                                                      vf[:, gg * T + ut * 128:
                                                         gg * T + (ut + 1) * 128],
                                                      ident_b[:])
                                ins.then_inc(S["PE_vt"], 1)

                        @blk.vector
                        def _(v):
                            for b in range(8):
                                v.wait_ge(S["PE_vt"], b + 1)
                                for j in range(8):
                                    i = b * 8 + j
                                    gg, ut = i // NT, i % NT
                                    cp = v.tensor_copy(
                                        vt[:, ut * 512 + gg * 128:
                                           ut * 512 + (gg + 1) * 128],
                                        tp1[b % 2][:, j * 128:(j + 1) * 128])
                                    if j == 7:
                                        cp.then_inc(S["DVE_vte"], 1)

                    if any(k in dbg for k in ("dbg_q", "dbg_k", "dbg_vt")):
                        with nc.Block() as blk:
                            @blk.gpsimd
                            def _(g):
                                if "dbg_q" in dbg:
                                    g.dma_start(dbg["dbg_q"].ap(), qf[:]
                                                ).then_inc(S["SY_dbg"], 16)
                                if "dbg_k" in dbg:
                                    g.dma_start(dbg["dbg_k"].ap(), kf[:]
                                                ).then_inc(S["SY_dbg"], 16)
                                if "dbg_vt" in dbg:
                                    g.dma_start(dbg["dbg_vt"].ap(), vt[:]
                                                ).then_inc(S["SY_dbg"], 16)

                with contextlib.ExitStack() as esa:
                    if not want("b3"):
                        raise StopBuild
                    SBa = lambda n, s, d: esa.enter_context(nc.sbuf_tensor(n, s, d))
                    PSa = lambda n, s, d: esa.enter_context(nc.psum_tensor(n, s, d))
                    bias_b = [SBa(f"bias{i}", [128, T], F32) for i in range(2)]
                    s_sb = [SBa(f"s_sb{i}", [128, T], F32) for i in range(2)]
                    p_sb = [SBa(f"p_sb{i}", [128, T], BF16) for i in range(2)]
                    pt_sb = [SBa(f"pt_sb{i}", [128, T], BF16) for i in range(2)]
                    nm = [SBa(f"nm{i}", [128, 1], F32) for i in range(2)]
                    sm = [SBa(f"sm{i}", [128, 1], F32) for i in range(2)]
                    rc = [SBa(f"rc{i}", [128, 1], F32) for i in range(4)]
                    stage = [SBa(f"stage{i}", [128, 128], BF16) for i in range(2)]
                    s_ps = [PSa(f"s_ps{i}", [128, 1024], F32) for i in range(2)]
                    mx2 = [SBa(f"mx2_{i}", [128, 2], F32) for i in range(2)]
                    tp3 = [PSa(f"tp3_{i}", [128, 1024], BF16) for i in range(2)]
                    av = [PSa(f"av{i}", [128, 128], F32) for i in range(2)]

                    NIT = H * NT

                    def nparts(it):
                        return 1 if (it % 16) < 8 else 2

                    def parts_upto(it):
                        return sum(nparts(i) for i in range(it + 1))

                    def batches_upto(it):
                        n = 0
                        for i in range(it + 1):
                            n += (i % 16 + 8) // 8
                        return n

                    def emit_tp_pe(t, pit):
                        pqt = pit % 16
                        t.wait_ge(S["ACT_ex"], pit + 1)
                        nb = (pqt + 8) // 8
                        b0 = batches_upto(pit - 1)
                        for b in range(nb):
                            gb = b0 + b
                            if gb >= 2:
                                t.wait_ge(S["DVE_te"], gb - 1)
                            for j in range(min(8, pqt + 1 - b * 8)):
                                ub = b * 8 + j
                                ins = t.transpose(
                                    tp3[gb % 2][:, j * 128:(j + 1) * 128],
                                    p_sb[pit % 2][:, ub * 128:(ub + 1) * 128],
                                    ident_b[:])
                            ins.then_inc(S["PE_tp"], 1)

                    def emit_av_pe(t, pit):
                        pqt = pit % 16
                        t.wait_ge(S["DVE_te"], batches_upto(pit))
                        if pit >= 2:
                            t.wait_ge(S["ACT_ae"], pit - 1)
                        pg = (pit // 16) % G
                        for ub in range(pqt + 1):
                            ins = t.matmul(
                                av[pit % 2][:],
                                pt_sb[pit % 2][:, ub * 128:(ub + 1) * 128],
                                vt[:, ub * 512 + pg * 128:ub * 512 + (pg + 1) * 128],
                                start=(ub == 0), stop=(ub == pqt))
                        ins.then_inc(S["PE_av"], 1)

                    with nc.Block() as blk:
                        @blk.sync
                        def _(sy):
                            for h in range(2):
                                sy.dma_start(bias_b[h][:],
                                             bass.AP(alibi_d, h * T, [[0, 128], [1, T]])
                                             ).then_inc(S["SY_bias"], 16)
                            for h in range(H):
                                for qt in range(NT):
                                    it = h * 16 + qt
                                    sy.wait_ge(S["ACT_ae"], it + 1)
                                    sy.dma_start(
                                        ao_s.ap()[qt * 128:(qt + 1) * 128,
                                                  h * 128:(h + 1) * 128],
                                        stage[it % 2][:]).then_inc(S["SY_ao"], 16)
                                if h + 2 < H:
                                    sy.wait_ge(S["DVE_ev"], (h + 1) * 24)
                                    sy.dma_start(
                                        bias_b[h % 2][:],
                                        bass.AP(alibi_d, (h + 2) * T, [[0, 128], [1, T]])
                                        ).then_inc(S["SY_bias"], 16)

                        @blk.tensor
                        def _(t):
                            for h in range(H):
                                for qt in range(NT):
                                    it = h * 16 + qt
                                    L = (qt + 1) * 128
                                    gs = h % G
                                    gp0 = parts_upto(it - 1)
                                    np_ = nparts(it)
                                    for p in range(np_):
                                        gp = gp0 + p
                                        pa = p * 1024
                                        pb = min(L, (p + 1) * 1024)
                                        if gp >= 2:
                                            t.wait_ge(S["DVE_ev"], gp - 1)
                                        last = p == np_ - 1
                                        u0 = pa
                                        while u0 < pb:
                                            ul = min(512, pb - u0)
                                            ins = t.matmul(
                                                s_ps[gp % 2][:, u0 - pa:u0 - pa + ul],
                                                qf[:, h * T + qt * 128:
                                                   h * T + (qt + 1) * 128],
                                                kf[:, gs * T + u0:gs * T + u0 + ul],
                                                start=True, stop=not last)
                                            u0 += ul
                                        if last:
                                            ins = t.matmul(
                                                s_ps[gp % 2][:, qt * 128 - pa:
                                                             (qt + 1) * 128 - pa],
                                                ident_b[:], mask_sb[:],
                                                start=False, stop=True)
                                        ins.then_inc(S["PE_s"], 1)
                                    if it >= 2:
                                        emit_tp_pe(t, it - 2)
                                    if it >= 3:
                                        emit_av_pe(t, it - 3)
                            for pit in (NIT - 2, NIT - 1):
                                emit_tp_pe(t, pit)
                            for pit in (NIT - 3, NIT - 2, NIT - 1):
                                emit_av_pe(t, pit)

                        @blk.vector
                        def _(v):
                            def do_recip(pit):
                                v.wait_ge(S["ACT_ex"], pit + 1)
                                if pit >= 4:
                                    v.wait_ge(S["ACT_ae"], pit - 3)
                                v.reciprocal(rc[pit % 4][:], sm[pit % 2][:]
                                             ).then_inc(S["DVE_rc"], 1)
                            for h in range(H):
                                for qt in range(NT):
                                    it = h * 16 + qt
                                    L = (qt + 1) * 128
                                    v.wait_ge(S["SY_bias"], 16 * (h + 1))
                                    if it >= 2:
                                        v.wait_ge(S["ACT_ex"], it - 1)
                                    gp0 = parts_upto(it - 1)
                                    np_ = nparts(it)
                                    for p in range(np_):
                                        gp = gp0 + p
                                        pa = p * 1024
                                        pb = min(L, (p + 1) * 1024)
                                        v.wait_ge(S["PE_s"], gp + 1)
                                        v.tensor_add(s_sb[it % 2][:, pa:pb],
                                                     s_ps[gp % 2][:, 0:pb - pa],
                                                     bias_b[h % 2][:, pa:pb]
                                                     ).then_inc(S["DVE_ev"], 1)
                                        v.reduce_max(mx2[it % 2][:, p:p + 1],
                                                     s_sb[it % 2][:, pa:pb],
                                                     axis=AX.X
                                                     ).then_inc(S["DVE_mxp"], 1)
                                    v.wait_ge(S["DVE_mxp"], gp0 + np_)
                                    v.reduce_max(nm[it % 2][:],
                                                 mx2[it % 2][:, 0:np_],
                                                 axis=AX.X, negate=True
                                                 ).then_inc(S["DVE_mx"], 1)
                                    if it >= 1:
                                        do_recip(it - 1)
                            do_recip(NIT - 1)

                        @blk.scalar
                        def _(sc):
                            def tpev_act(sc, pit):
                                pqt = pit % 16
                                nb = (pqt + 8) // 8
                                b0 = batches_upto(pit - 1)
                                if pit >= 2:
                                    sc.wait_ge(S["PE_av"], pit - 1)
                                for b in range(nb):
                                    gb = b0 + b
                                    sc.wait_ge(S["PE_tp"], gb + 1)
                                    nj = min(8, pqt + 1 - b * 8)
                                    sc.copy(
                                        pt_sb[pit % 2][:, b * 1024:b * 1024 + nj * 128],
                                        tp3[gb % 2][:, 0:nj * 128]
                                        ).then_inc(S["DVE_te"], 1)

                            def avev_act(sc, pit):
                                sc.wait_ge(S["PE_av"], pit + 1)
                                sc.wait_ge(S["DVE_rc"], pit + 1)
                                if pit >= 2:
                                    sc.wait_ge(S["SY_ao"], 16 * (pit - 1))
                                sc.mul(stage[pit % 2][:], av[pit % 2][:],
                                       rc[pit % 4][:]).then_inc(S["ACT_ae"], 1)

                            def exp_act(sc, pit):
                                L = (pit % 16 + 1) * 128
                                sc.wait_ge(S["DVE_mx"], pit + 1)
                                if pit >= 2:
                                    sc.wait_ge(S["PE_tp"], batches_upto(pit - 2))
                                    sc.wait_ge(S["DVE_rc"], pit - 1)
                                sc.activation(p_sb[pit % 2][:, 0:L],
                                              s_sb[pit % 2][:, 0:L],
                                              AF.Exp, bias=nm[pit % 2][:], scale=1.0,
                                              accum_out=sm[pit % 2][:]
                                              ).then_inc(S["ACT_ex"], 1)

                            for it in range(NIT):
                                if it >= 1:
                                    exp_act(sc, it - 1)
                                if it >= 2:
                                    tpev_act(sc, it - 2)
                                if it >= 3:
                                    avev_act(sc, it - 3)
                            exp_act(sc, NIT - 1)
                            for pit in (NIT - 2, NIT - 1):
                                tpev_act(sc, pit)
                            for pit in (NIT - 3, NIT - 2, NIT - 1):
                                avev_act(sc, pit)

            if "dbg_ao" in dbg:
                with nc.Block() as blk:
                    @blk.gpsimd
                    def _(g):
                        g.wait_ge(S["SY_ao"], 16 * 256)
                        g.dma_start(dbg["dbg_ao"].ap(), ao_s.ap()
                                    ).then_inc(S["SY_dbg"], 16)

            with contextlib.ExitStack() as esw:
                if not want("b4"):
                    raise StopBuild
                SBw = lambda n, s, d: esw.enter_context(nc.sbuf_tensor(n, s, d))
                PSw = lambda n, s, d: esw.enter_context(nc.psum_tensor(n, s, d))
                wo_sb = SBw("wo_sb", [128, KC * C], BF16)
                aot = [SBw(f"aot{i}", [128, C], BF16) for i in range(2)]
                aoT = [SBw(f"aoT{i}", [128, C], BF16) for i in range(2)]
                xres = [SBw(f"xres{i}", [128, C], F32) for i in range(2)]
                r1 = [SBw(f"r1_{i}", [128, C], F32) for i in range(2)]
                y1 = [SBw(f"y1_{i}", [128, C], F32) for i in range(2)]
                stats = SBw("stats", [128, 24], F32)
                aggr = SBw("aggr", [128, 2], F32)
                vecol = SBw("vecol", [128, 1], F32)
                lncol = SBw("lncol", [128, 1], F32)
                rstd = SBw("rstd", [128, 1], F32)
                wo_ps = PSw("wo_ps", [128, C], F32)
                tp4 = [PSw(f"tp4_{i}", [128, 1024], BF16) for i in range(2)]

                with nc.Block() as blk:
                    @blk.gpsimd
                    def _(g):
                        for Q in range(4):
                            src = wo_d.ap()[:, Q * 512:(Q + 1) * 512]
                            src = src.rearrange("(kc p) f -> p kc f", p=128)
                            dst = wo_sb[:].rearrange("p (kc f) -> p kc f", kc=KC)
                            g.dma_start(dst[:, :, Q * 512:(Q + 1) * 512], src
                                        ).then_inc(S["GP_w"], 16)
                        for qt in range(NT):
                            g.wait_ge(S["DVE_y1"], qt + 1)
                            g.dma_start(y1b_s.ap()[qt * 128:(qt + 1) * 128, :],
                                        y1[qt % 2][:]).then_inc(S["GP_y1b"], 16)

                    @blk.sync
                    def _(sy):
                        sy.wait_ge(S["SY_ao"], 16 * 256)
                        for qt in range(NT):
                            if qt >= 2:
                                sy.wait_ge(S["PE_t5"], 2 * qt - 2)
                            sy.dma_start(aot[qt % 2][:],
                                         ao_s.ap()[qt * 128:(qt + 1) * 128, :]
                                         ).then_inc(S["SY_aot"], 16)
                            if qt >= 2:
                                sy.wait_ge(S["DVE_r1"], qt - 1)
                            sy.dma_start(xres[qt % 2][:],
                                         x_d.ap()[qt * 128:(qt + 1) * 128, :]
                                         ).then_inc(S["SY_xres"], 16)
                            if qt >= 1:
                                sy.wait_ge(S["DVE_y1"], qt)
                                sy.dma_start(y1f_s.ap()[(qt - 1) * 128:qt * 128, :],
                                             y1[(qt - 1) % 2][:]
                                             ).then_inc(S["SY_y1f"], 16)
                        sy.wait_ge(S["DVE_y1"], NT)
                        sy.dma_start(y1f_s.ap()[(NT - 1) * 128:NT * 128, :],
                                     y1[(NT - 1) % 2][:]).then_inc(S["SY_y1f"], 16)

                    @blk.tensor
                    def _(t):
                        t.wait_ge(S["GP_w"], 16 * 16)
                        for qt in range(NT):
                            t.wait_ge(S["SY_aot"], 16 * (qt + 1))
                            for b in range(2):
                                gb = qt * 2 + b
                                if gb >= 2:
                                    t.wait_ge(S["DVE_te5"], gb - 1)
                                for j in range(8):
                                    c = b * 8 + j
                                    ins = t.transpose(
                                        tp4[gb % 2][:, j * 128:(j + 1) * 128],
                                        aot[qt % 2][:, c * 128:(c + 1) * 128],
                                        ident_b[:])
                                ins.then_inc(S["PE_t5"], 1)
                            t.wait_ge(S["DVE_te5"], qt * 2 + 2)
                            if qt >= 1:
                                t.wait_ge(S["DVE_r1"], qt)
                            for sp in range(4):
                                for c in range(KC):
                                    t.matmul(wo_ps[:, sp * 512:(sp + 1) * 512],
                                             aoT[qt % 2][:, c * 128:(c + 1) * 128],
                                             wo_sb[:, c * C + sp * 512:
                                                   c * C + (sp + 1) * 512],
                                             start=(c == 0), stop=False)
                                t.matmul(wo_ps[:, sp * 512:(sp + 1) * 512], ones_b[:],
                                         bo_hi[:, sp * 512:(sp + 1) * 512],
                                         start=False, stop=False)
                                ins = t.matmul(wo_ps[:, sp * 512:(sp + 1) * 512],
                                               ones_b[:],
                                               bo_lo[:, sp * 512:(sp + 1) * 512],
                                               start=False, stop=True)
                            ins.then_inc(S["PE_wo"], 1)

                    @blk.vector
                    def _(v):
                        for qt in range(NT):
                            for b in range(2):
                                gb = qt * 2 + b
                                v.wait_ge(S["PE_t5"], gb + 1)
                                if qt >= 2:
                                    v.wait_ge(S["PE_wo"], qt - 1)
                                v.tensor_copy(aoT[qt % 2][:, b * 1024:(b + 1) * 1024],
                                              tp4[gb % 2][:]).then_inc(S["DVE_te5"], 1)
                            v.wait_ge(S["PE_wo"], qt + 1)
                            v.wait_ge(S["SY_xres"], 16 * (qt + 1))
                            if qt >= 2:
                                v.wait_ge(S["GP_y1b"], 16 * (qt - 1))
                                v.wait_ge(S["SY_y1f"], 16 * (qt - 1))
                            v.tensor_add(r1[qt % 2][:], wo_ps[:], xres[qt % 2][:]
                                         ).then_inc(S["DVE_r1"], 1)
                            for i in range(4):
                                v.bn_stats(stats[:, 6 * i:6 * (i + 1)],
                                           r1[qt % 2][:, 512 * i:512 * (i + 1)]
                                           ).then_inc(S["DVE_bn"], 1)
                            v.wait_ge(S["DVE_bn"], 6 * qt + 4)
                            v.bn_aggr(aggr[:], stats[:]).then_inc(S["DVE_bn"], 1)
                            v.wait_ge(S["DVE_bn"], 6 * qt + 5)
                            v.tensor_scalar_add(vecol[:], aggr[:, 1:2], EPS
                                                ).then_inc(S["DVE_bn"], 1)
                            v.wait_ge(S["ACT_rs"], 2 * qt + 2)
                            v.tensor_scalar(y1[qt % 2][:], r1[qt % 2][:],
                                            aggr[:, 0:1], rstd[:],
                                            ALU.subtract, ALU.mult
                                            ).then_inc(S["DVE_y1"], 1)

                    @blk.scalar
                    def _(sc):
                        for qt in range(NT):
                            sc.wait_ge(S["DVE_bn"], 6 * qt + 6)
                            sc.activation(lncol[:], vecol[:], AF.Ln
                                          ).then_inc(S["ACT_rs"], 1)
                            sc.wait_ge(S["ACT_rs"], 2 * qt + 1)
                            sc.activation(rstd[:], lncol[:], AF.Exp, scale=-0.5
                                          ).then_inc(S["ACT_rs"], 1)

            if "dbg_y1" in dbg:
                with nc.Block() as blk:
                    @blk.sync
                    def _(sy):
                        sy.wait_ge(S["SY_y1f"], 16 * NT)
                        sy.dma_start(dbg["dbg_y1"].ap(), y1f_s.ap()
                                     ).then_inc(S["SY_dbg"], 16)

            with contextlib.ExitStack() as esy:
                y1T = esy.enter_context(nc.sbuf_tensor("y1T", [128, KC * T], BF16))

                with contextlib.ExitStack() as esr:
                    if not want("b6"):
                        raise StopBuild
                    SBr = lambda n, s, d: esr.enter_context(nc.sbuf_tensor(n, s, d))
                    PSr = lambda n, s, d: esr.enter_context(nc.psum_tensor(n, s, d))
                    y1b = [SBr(f"y1b{i}", [128, C], BF16) for i in range(2)]
                    lg = SBr("lg", [128, E], F32)
                    top8 = SBr("top8", [128, E], F32)
                    idx_u = SBr("idx_u", [128, E], U32)
                    idx_f = SBr("idx_f", [128, 2], F32)
                    nm1 = SBr("rt_nm1", [128, 1], F32)
                    e2c = SBr("e2c", [128, 1], F32)
                    dnc = SBr("dnc", [128, 1], F32)
                    w1c = SBr("w1c", [128, 1], F32)
                    w2c = SBr("w2c", [128, 1], F32)
                    oh1 = SBr("oh1", [128, E], F32)
                    oh2 = SBr("oh2", [128, E], F32)
                    combt = SBr("combt", [128, E], F32)
                    comb_sb = SBr("comb_sb", [8, T], F32)
                    tp5 = [PSr(f"tp5_{i}", [128, 1024], BF16) for i in range(2)]
                    gate_ps = PSr("gate_ps", [128, E], F32)
                    ct_ps = PSr("ct_ps", [128, 128], F32)

                    with nc.Block() as blk:
                        @blk.sync
                        def _(sy):
                            sy.wait_ge(S["GP_y1b"], 16 * NT)
                            for tt in range(NT):
                                if tt >= 2:
                                    sy.wait_ge(S["PE_gate"], tt - 1)
                                sy.dma_start(y1b[tt % 2][:],
                                             y1b_s.ap()[tt * 128:(tt + 1) * 128, :]
                                             ).then_inc(S["SY_y1rd"], 16)
                            sy.wait_ge(S["DVE_cte"], NT)
                            sy.dma_start(comb_s.ap(), comb_sb[:]
                                         ).then_inc(S["SY_comb"], 16)

                        @blk.tensor
                        def _(t):
                            for tt in range(NT):
                                t.wait_ge(S["SY_y1rd"], 16 * (tt + 1))
                                for b in range(2):
                                    gb = tt * 2 + b
                                    if gb >= 2:
                                        t.wait_ge(S["DVE_te5"], 32 + gb - 1)
                                    for j in range(8):
                                        c = b * 8 + j
                                        ins = t.transpose(
                                            tp5[gb % 2][:, j * 128:(j + 1) * 128],
                                            y1b[tt % 2][:, c * 128:(c + 1) * 128],
                                            ident_b[:])
                                    ins.then_inc(S["PE_t5"], 1)
                                t.wait_ge(S["DVE_te5"], 32 + tt * 2 + 2)
                                if tt >= 1:
                                    t.wait_ge(S["DVE_rt"], (tt - 1) * 11 + 1)
                                for kc in range(KC):
                                    ins = t.matmul(
                                        gate_ps[:],
                                        y1T[:, kc * T + tt * 128:
                                           kc * T + (tt + 1) * 128],
                                        gate_sb[:, kc * E:(kc + 1) * E],
                                        start=(kc == 0), stop=(kc == KC - 1))
                                ins.then_inc(S["PE_gate"], 1)
                                t.wait_ge(S["DVE_rt"], (tt + 1) * 11)
                                if tt >= 1:
                                    t.wait_ge(S["DVE_cte"], tt)
                                t.transpose(ct_ps[0:E, :], combt[:], ident_f[:]
                                            ).then_inc(S["PE_ct"], 1)

                        @blk.vector
                        def _(v):
                            y1Tv = y1T[:].rearrange("p (kc t) -> p kc t", kc=KC)
                            for tt in range(NT):
                                for b in range(2):
                                    gb = tt * 2 + b
                                    v.wait_ge(S["PE_t5"], 32 + gb + 1)
                                    src = tp5[gb % 2][:].rearrange(
                                        "p (j t) -> p j t", j=8)
                                    dst = y1Tv[:, b * 8:b * 8 + 8,
                                               tt * 128:(tt + 1) * 128]
                                    v.tensor_copy(dst, src).then_inc(S["DVE_te5"], 1)
                                v.wait_ge(S["PE_gate"], tt + 1)
                                if tt >= 1:
                                    v.wait_ge(S["PE_ct"], tt)
                                v.tensor_copy(lg[:], gate_ps[:]).then_inc(S["DVE_rt"], 1)
                                v.wait_ge(S["DVE_rt"], tt * 11 + 1)
                                v.max(top8[:], lg[:]).then_inc(S["DVE_rt"], 1)
                                v.wait_ge(S["DVE_rt"], tt * 11 + 2)
                                v.max_index(idx_u[:], top8[:], lg[:]
                                            ).then_inc(S["DVE_rt"], 1)
                                v.wait_ge(S["DVE_rt"], tt * 11 + 3)
                                v.tensor_copy(idx_f[:], idx_u[:, 0:2]
                                              ).then_inc(S["DVE_rt"], 1)
                                v.tensor_scalar_mul(nm1[:], top8[:, 0:1], -1.0
                                                    ).then_inc(S["DVE_rt"], 1)
                                v.wait_ge(S["ACT_rt"], tt + 1)
                                v.tensor_scalar_add(dnc[:], e2c[:], 1.0
                                                    ).then_inc(S["DVE_rt"], 1)
                                v.wait_ge(S["DVE_rt"], tt * 11 + 6)
                                v.reciprocal(w1c[:], dnc[:]).then_inc(S["DVE_rt"], 1)
                                v.wait_ge(S["DVE_rt"], tt * 11 + 7)
                                v.tensor_mul(w2c[:], e2c[:], w1c[:]
                                             ).then_inc(S["DVE_rt"], 1)
                                v.wait_ge(S["DVE_rt"], tt * 11 + 8)
                                v.tensor_scalar(oh1[:], iota8[:], idx_f[:, 0:1], None,
                                                ALU.is_equal)
                                v.tensor_scalar(oh2[:], iota8[:], idx_f[:, 1:2], None,
                                                ALU.is_equal).then_inc(S["DVE_rt"], 1)
                                v.wait_ge(S["DVE_rt"], tt * 11 + 9)
                                v.tensor_scalar_mul(oh1[:], oh1[:], w1c[:])
                                v.tensor_scalar_mul(oh2[:], oh2[:], w2c[:]
                                                    ).then_inc(S["DVE_rt"], 1)
                                v.wait_ge(S["DVE_rt"], tt * 11 + 10)
                                v.tensor_add(combt[:], oh1[:], oh2[:]
                                             ).then_inc(S["DVE_rt"], 1)
                                v.wait_ge(S["PE_ct"], tt + 1)
                                v.tensor_copy(comb_sb[0:E, tt * 128:(tt + 1) * 128],
                                              ct_ps[0:E, :]).then_inc(S["DVE_cte"], 1)

                        @blk.scalar
                        def _(sc):
                            for tt in range(NT):
                                sc.wait_ge(S["DVE_rt"], tt * 11 + 5)
                                sc.activation(e2c[:], top8[:, 1:2], AF.Exp,
                                              bias=nm1[:], scale=1.0
                                              ).then_inc(S["ACT_rt"], 1)

                if "dbg_comb" in dbg:
                    with nc.Block() as blk:
                        @blk.sync
                        def _(sy):
                            sy.wait_ge(S["SY_comb"], 16)
                            sy.dma_start(dbg["dbg_comb"].ap(), comb_s.ap()
                                         ).then_inc(S["SY_dbg"], 16)

                with contextlib.ExitStack() as esf:
                    if not want("b7"):
                        raise StopBuild
                    SBf = lambda n, s, d: esf.enter_context(nc.sbuf_tensor(n, s, d))
                    PSf = lambda n, s, d: esf.enter_context(nc.psum_tensor(n, s, d))
                    cb = [SBf(f"cb{i}", [128, T], F32) for i in range(2)]
                    wgb = [SBf(f"wgb{i}", [128, KC * 128], BF16) for i in range(2)]
                    wub = [SBf(f"wub{i}", [128, KC * 128], BF16) for i in range(2)]
                    sg = [SBf(f"sg{i}", [128, 1024], BF16) for i in range(2)]
                    uc = SBf("uc", [128, 1024], F32)
                    ab = [SBf(f"ab{i}", [128, 1024], BF16) for i in range(2)]
                    wdnb = [SBf(f"wdnb{i}", [128, C], BF16) for i in range(2)]
                    g_ps = [PSf(f"g_ps{i}", [128, 1024], F32) for i in range(2)]
                    u_ps = [PSf(f"u_ps{i}", [128, 1024], F32) for i in range(2)]

                    with nc.Block() as blk:
                        @blk.sync
                        def _(sy):
                            sy.wait_ge(S["SY_comb"], 16)
                            sy.dma_start(cb[0][:],
                                         bass.AP(comb_s, 0, [[0, 128], [1, T]])
                                         ).then_inc(S["SY_cb"], 16)
                            sy.dma_start(cb[1][:],
                                         bass.AP(comb_s, T, [[0, 128], [1, T]])
                                         ).then_inc(S["SY_cb"], 16)
                            for e in range(E):
                                if 2 <= e:
                                    pass
                                for mc in range(MC):
                                    j = e * MC + mc
                                    sy.wait_ge(S["GP_wdn"], 16 * (j + 1))
                                    sy.dma_start(wdn_s.ap()[e, mc], wdnb[j % 2][:]
                                                 ).then_inc(S["SY_wdno"], 16)
                                    for th in range(2):
                                        it = j * 2 + th
                                        sy.wait_ge(S["DVE_a"], it + 1)
                                        dst = a_s.ap()[th * 8:(th + 1) * 8, :,
                                                       e:e + 1, mc:mc + 1, :]
                                        dst = dst.rearrange(
                                            "tt f e m t -> f tt (e m t)")
                                        src = ab[it % 2][:].rearrange(
                                            "f (tt t) -> f tt t", tt=8)
                                        sy.dma_start(dst, src).then_inc(S["SY_aout"], 16)
                                if e + 2 < E:
                                    sy.wait_ge(S["DVE_a"], (e + 1) * 2 * MC)
                                    sy.dma_start(cb[e % 2][:],
                                                 bass.AP(comb_s, (e + 2) * T,
                                                         [[0, 128], [1, T]])
                                                 ).then_inc(S["SY_cb"], 16)

                        @blk.gpsimd
                        def _(g):
                            for e in range(E):
                                for mc in range(MC):
                                    j = e * MC + mc
                                    if j >= 2:
                                        g.wait_ge(S["PE_f1"], 2 * (j - 1))
                                    src = wg_d.ap()[e, :, mc * 128:(mc + 1) * 128]
                                    src = src.rearrange("(kc p) m -> p kc m", p=128)
                                    dst = wgb[j % 2][:].rearrange(
                                        "p (kc m) -> p kc m", kc=KC)
                                    g.dma_start(dst, src).then_inc(S["GP_gw"], 16)
                                    src = wu_d.ap()[e, :, mc * 128:(mc + 1) * 128]
                                    src = src.rearrange("(kc p) m -> p kc m", p=128)
                                    dst = wub[j % 2][:].rearrange(
                                        "p (kc m) -> p kc m", kc=KC)
                                    g.dma_start(dst, src).then_inc(S["GP_gw"], 16)
                                    if j >= 2:
                                        g.wait_ge(S["SY_wdno"], 16 * (j - 1))
                                    g.dma_start(wdnb[j % 2][:],
                                                wd_d.ap()[e, mc * 128:(mc + 1) * 128, :]
                                                ).then_inc(S["GP_wdn"], 16)

                        @blk.tensor
                        def _(t):
                            for e in range(E):
                                for mc in range(MC):
                                    j = e * MC + mc
                                    t.wait_ge(S["GP_gw"], 32 * (j + 1))
                                    for th in range(2):
                                        it = j * 2 + th
                                        if it >= 2:
                                            t.wait_ge(S["ACT_si"], it - 1)
                                            t.wait_ge(S["DVE_uc"], it - 1)
                                        for sp in range(2):
                                            t0 = th * 1024 + sp * 512
                                            for kc in range(KC):
                                                t.matmul(
                                                    g_ps[it % 2][:, sp * 512:(sp + 1) * 512],
                                                    wgb[j % 2][:, kc * 128:(kc + 1) * 128],
                                                    y1T[:, kc * T + t0:kc * T + t0 + 512],
                                                    start=(kc == 0), stop=(kc == KC - 1))
                                            for kc in range(KC):
                                                ins = t.matmul(
                                                    u_ps[it % 2][:, sp * 512:(sp + 1) * 512],
                                                    wub[j % 2][:, kc * 128:(kc + 1) * 128],
                                                    y1T[:, kc * T + t0:kc * T + t0 + 512],
                                                    start=(kc == 0), stop=(kc == KC - 1))
                                        ins.then_inc(S["PE_f1"], 1)

                        @blk.scalar
                        def _(sc):
                            for e in range(E):
                                for mc in range(MC):
                                    j = e * MC + mc
                                    for th in range(2):
                                        it = j * 2 + th
                                        sc.wait_ge(S["PE_f1"], it + 1)
                                        if it >= 2:
                                            sc.wait_ge(S["DVE_a"], it - 1)
                                        sc.activation(sg[it % 2][:], g_ps[it % 2][:],
                                                      AF.Silu).then_inc(S["ACT_si"], 1)

                        @blk.vector
                        def _(v):
                            for e in range(E):
                                for mc in range(MC):
                                    j = e * MC + mc
                                    for th in range(2):
                                        it = j * 2 + th
                                        v.wait_ge(S["PE_f1"], it + 1)
                                        v.wait_ge(S["SY_cb"], 16 * (e + 1))
                                        v.tensor_mul(uc[:], u_ps[it % 2][:],
                                                     cb[e % 2][:, th * 1024:(th + 1) * 1024]
                                                     ).then_inc(S["DVE_uc"], 1)
                                        v.wait_ge(S["ACT_si"], it + 1)
                                        if it >= 2:
                                            v.wait_ge(S["SY_aout"], 16 * (it - 1))
                                        v.tensor_mul(ab[it % 2][:], uc[:], sg[it % 2][:]
                                                     ).then_inc(S["DVE_a"], 1)

            with contextlib.ExitStack() as es8:
                if not want("b8"):
                    raise StopBuild
                SB8 = lambda n, s, d: es8.enter_context(nc.sbuf_tensor(n, s, d))
                PS8 = lambda n, s, d: es8.enter_context(nc.psum_tensor(n, s, d))
                NE = E * MC
                a_sb = SB8("a_sb", [128, 4 * NE * 128], BF16)
                wdn2 = [SB8(f"wdn2_{i}", [128, 2048], BF16) for i in range(4)]
                y1res = [SB8(f"y1res{i}", [128, C], F32) for i in range(4)]
                r2 = [SB8(f"r2_{i}", [128, C], F32) for i in range(4)]
                outb = [SB8(f"outb{i}", [128, C], F32) for i in range(2)]
                stats2 = SB8("stats2", [128, 24], F32)
                aggr2 = SB8("aggr2", [128, 2], F32)
                vecol2 = SB8("vecol2", [128, 1], F32)
                lncol2 = SB8("lncol2", [128, 1], F32)
                rstd2 = SB8("rstd2", [128, 1], F32)
                ffn_ps = [PS8(f"ffn_ps{i}", [128, 1024], F32) for i in range(4)]

                with nc.Block() as blk:
                    @blk.sync
                    def _(sy):
                        sy.wait_ge(S["SY_aout"], 16 * 2 * NE)
                        sy.wait_ge(S["SY_wdno"], 16 * NE)
                        for tb in range(4):
                            if tb >= 1:
                                sy.wait_ge(S["PE_f2"], tb * NE)
                            for ttl in range(4):
                                tt = tb * 4 + ttl
                                dst = a_sb[:, ttl * NE * 128:(ttl + 1) * NE * 128]
                                src = a_s.ap()[tt:tt + 1].rearrange(
                                    "z f e m t -> (z f) (e m t)")
                                sy.dma_start(dst, src).then_inc(S["SY_ain"], 16)
                                if tb >= 1:
                                    sy.wait_ge(S["DVE_f2"], (tb - 1) * 8 + 4 + ttl + 1)
                                sy.dma_start(y1res[ttl][:],
                                             y1f_s.ap()[tt * 128:(tt + 1) * 128, :]
                                             ).then_inc(S["SY_y1res"], 16)
                            wdn_flat = wdn_s.ap().rearrange("e m f c -> (e m) f c")
                            for hh in range(2):
                                for pj in range(NE // 2):
                                    gp = (tb * 2 + hh) * (NE // 2) + pj
                                    if gp % 2:
                                        continue
                                    if gp >= 4:
                                        sy.wait_ge(S["PE_f2"], gp - 3)
                                    src = wdn_flat[2 * pj:2 * pj + 2, :,
                                                   hh * 1024:(hh + 1) * 1024]
                                    src = src.rearrange("m f c -> f m c")
                                    dst = wdn2[gp % 4][:].rearrange(
                                        "f (m c) -> f m c", m=2)
                                    sy.dma_start(dst, src).then_inc(S["SY_wdn2"], 16)
                            for ttl in range(4):
                                tt = tb * 4 + ttl
                                sy.wait_ge(S["DVE_o"], tt + 1)
                                sy.dma_start(out_d.ap()[tt * 128:(tt + 1) * 128, :],
                                             outb[ttl % 2][:]).then_inc(S["SY_out"], 16)
                        sy.wait_ge(S["SY_out"], 16 * NT)

                    @blk.gpsimd
                    def _(g):
                        g.wait_ge(S["SY_wdno"], 16 * NE)
                        wdn_flat = wdn_s.ap().rearrange("e m f c -> (e m) f c")
                        for tb in range(4):
                            for hh in range(2):
                                for pj in range(NE // 2):
                                    gp = (tb * 2 + hh) * (NE // 2) + pj
                                    if gp % 2 == 0:
                                        continue
                                    if gp >= 4:
                                        g.wait_ge(S["PE_f2"], gp - 3)
                                    src = wdn_flat[2 * pj:2 * pj + 2, :,
                                                   hh * 1024:(hh + 1) * 1024]
                                    src = src.rearrange("m f c -> f m c")
                                    dst = wdn2[gp % 4][:].rearrange(
                                        "f (m c) -> f m c", m=2)
                                    g.dma_start(dst, src).then_inc(S["GP_wdn"], 16)

                    @blk.tensor
                    def _(t):
                        gwdn_base = 16 * NE
                        for tb in range(4):
                            t.wait_ge(S["SY_ain"], 16 * (tb * 4 + 4))
                            for hh in range(2):
                                rnd = tb * 2 + hh
                                if rnd >= 1:
                                    t.wait_ge(S["DVE_f2"], rnd * 4)
                                for pj in range(NE // 2):
                                    gp = rnd * (NE // 2) + pj
                                    if gp % 2 == 0:
                                        t.wait_ge(S["SY_wdn2"], 16 * (gp // 2 + 1))
                                    else:
                                        t.wait_ge(S["GP_wdn"],
                                                  gwdn_base + 16 * (gp // 2 + 1))
                                    for jin in range(2):
                                        jj = 2 * pj + jin
                                        for ttl in range(4):
                                            for sp in range(2):
                                                ins = t.matmul(
                                                    ffn_ps[ttl][:, sp * 512:(sp + 1) * 512],
                                                    a_sb[:, (ttl * NE + jj) * 128:
                                                         (ttl * NE + jj + 1) * 128],
                                                    wdn2[gp % 4][:, jin * 1024 + sp * 512:
                                                         jin * 1024 + (sp + 1) * 512],
                                                    start=(jj == 0), stop=(jj == NE - 1))
                                    ins.then_inc(S["PE_f2"], 1)

                    @blk.vector
                    def _(v):
                        for tb in range(4):
                            for hh in range(2):
                                rnd = tb * 2 + hh
                                v.wait_ge(S["PE_f2"], (rnd + 1) * (NE // 2))
                                for ttl in range(4):
                                    v.wait_ge(S["SY_y1res"], 16 * (tb * 4 + ttl + 1))
                                    v.tensor_add(r2[ttl][:, hh * 1024:(hh + 1) * 1024],
                                                 ffn_ps[ttl][:],
                                                 y1res[ttl][:, hh * 1024:(hh + 1) * 1024]
                                                 ).then_inc(S["DVE_f2"], 1)
                            for ttl in range(4):
                                tt = tb * 4 + ttl
                                v.wait_ge(S["DVE_f2"], tb * 8 + 4 + ttl + 1)
                                for i in range(4):
                                    v.bn_stats(stats2[:, 6 * i:6 * (i + 1)],
                                               r2[ttl][:, 512 * i:512 * (i + 1)]
                                               ).then_inc(S["DVE_ln2"], 1)
                                v.wait_ge(S["DVE_ln2"], 6 * tt + 4)
                                v.bn_aggr(aggr2[:], stats2[:]).then_inc(S["DVE_ln2"], 1)
                                v.wait_ge(S["DVE_ln2"], 6 * tt + 5)
                                v.tensor_scalar_add(vecol2[:], aggr2[:, 1:2], EPS
                                                    ).then_inc(S["DVE_ln2"], 1)
                                v.wait_ge(S["ACT_rs2"], 2 * tt + 2)
                                if tt >= 2:
                                    v.wait_ge(S["SY_out"], 16 * (tt - 1))
                                v.tensor_scalar(outb[ttl % 2][:], r2[ttl][:],
                                                aggr2[:, 0:1], rstd2[:],
                                                ALU.subtract, ALU.mult
                                                ).then_inc(S["DVE_o"], 1)

                    @blk.scalar
                    def _(sc):
                        for tb in range(4):
                            for ttl in range(4):
                                tt = tb * 4 + ttl
                                sc.wait_ge(S["DVE_ln2"], 6 * tt + 6)
                                sc.activation(lncol2[:], vecol2[:], AF.Ln
                                              ).then_inc(S["ACT_rs2"], 1)
                                sc.wait_ge(S["ACT_rs2"], 2 * tt + 1)
                                sc.activation(rstd2[:], lncol2[:], AF.Exp, scale=-0.5
                                              ).then_inc(S["ACT_rs2"], 1)

    except StopBuild:
        pass
    return nc


def make_consts():
    start = 2.0 ** (-2.0 ** (-(math.log2(H) - 3.0)))
    slopes = np.array([start ** (i + 1) for i in range(H)], dtype=np.float64)
    alibi = (slopes[:, None] * np.arange(T, dtype=np.float64)[None, :])
    import ml_dtypes
    mask = np.triu(np.full((128, 128), NEG, dtype=np.float32), k=1)
    return alibi.astype(np.float32), mask.astype(ml_dtypes.bfloat16)


_NC_CACHE = {}


def _get_nc(debug=(), stop="b8"):
    key = (tuple(sorted(debug)), stop)
    if key not in _NC_CACHE:
        _NC_CACHE[key] = build(key[0], stop)
    return _NC_CACHE[key]


def kernel(L0_attn_x, L0_attn_Wq, L0_attn_Wkv, L0_attn_Wo, L0_attn_bo,
           L0_ffn_gate_w, L0_ffn_w_gate, L0_ffn_w_up, L0_ffn_w_down,
           _debug=(), _stop="b8"):
    x = np.ascontiguousarray(np.asarray(L0_attn_x, dtype=np.float32))
    alibi, mask_b = make_consts()
    common = {
        "wq": np.ascontiguousarray(np.asarray(L0_attn_Wq, dtype=np.float32)),
        "wkv": np.ascontiguousarray(np.asarray(L0_attn_Wkv, dtype=np.float32)),
        "wo": np.ascontiguousarray(np.asarray(L0_attn_Wo, dtype=np.float32)),
        "bo": np.ascontiguousarray(
            np.asarray(L0_attn_bo, dtype=np.float32).reshape(1, C)),
        "gw": np.ascontiguousarray(np.asarray(L0_ffn_gate_w, dtype=np.float32)),
        "wg": np.ascontiguousarray(np.asarray(L0_ffn_w_gate, dtype=np.float32)),
        "wu": np.ascontiguousarray(np.asarray(L0_ffn_w_up, dtype=np.float32)),
        "wd": np.ascontiguousarray(np.asarray(L0_ffn_w_down, dtype=np.float32)),
        "alibi": alibi,
        "mask": mask_b,
    }
    in_maps = [dict(common, x=np.ascontiguousarray(x[i])) for i in range(8)]
    nc = _get_nc(_debug, _stop)
    res = run_bass_kernel_spmd(nc, in_maps, core_ids=list(range(8)))
    if _debug:
        return res
    return np.stack([r["out"] for r in res.results], axis=0)
